# version 18
# baseline (speedup 1.0000x reference)
import numpy as np
import ml_dtypes
from contextlib import ExitStack

import concourse.bass as bass
import concourse.mybir as mybir
from concourse.bass_utils import run_bass_kernel_spmd
from concourse.alu_op_type import AluOpType as ALU

F32 = mybir.dt.float32
BF16 = mybir.dt.bfloat16
FP8 = mybir.dt.float8e4
AF = mybir.ActivationFunctionType
AX = mybir.AxisListType.X

T = 16384
C = 2048
CONSTVALS = [0.0, 1.0, -0.5, -1.0, 1e-5, 1e-6, 0.5, 2.0]
NCH = 8
RT = 16
NELW = 2048 * 8192

ENGS = ("sync", "scalar", "gpsimd", "tensor", "vector")
ESEMS = {"tensor": "sPE", "vector": "sVE", "scalar": "sSE", "gpsimd": "sPO"}
DSEMS = (["dPRO", "dGNW", "dGAM", "dSTG0", "dSTG1", "dSCR0", "dSCR1",
          "dSCR2", "dSCR3",
          "dROPE", "dXF0", "dXF1", "dXT0", "dXT1", "dXRES", "dWF0", "dWF1",
          "dWP0", "dWP1", "dOUT", "dF0", "dF1", "dFR", "dOX1", "dOAC",
          "dSTAT", "dWCL", "dWCL1"]
         + ["dX2T_%d" % i for i in range(8)]
         + ["dXTs0", "dXTs1", "dXTp0", "dXTp1"]
         + ["dX2N_%d" % i for i in range(16)]
         + ["dXB%d_%d" % (j, p) for j in range(8) for p in range(2)])
ALLSEMS = tuple(ESEMS.values()) + tuple(DSEMS)


class Sch:
    def __init__(self):
        self.c = {s: 0 for s in ALLSEMS}
        self.ev = {e: [] for e in ENGS}
        self.lastw = {e: {s: 0 for s in ALLSEMS} for e in ENGS}

    def emit(self, eng, fn, inc=None, waits=()):
        if fn is not None and eng in ("vector", "scalar", "gpsimd"):
            waits = list(waits) + [(ESEMS[eng], self.c[ESEMS[eng]])]
        w = []
        for (s, v) in waits:
            if v > self.lastw[eng][s]:
                w.append((s, v))
                self.lastw[eng][s] = v
        if inc is not None:
            self.c[inc] += 16 if inc.startswith("d") else 1
        self.ev[eng].append((w, fn, inc))

    def now(self, s):
        return self.c[s]

    def barrier(self, snap=None):
        if snap is None:
            snap = dict(self.c)
        esems = tuple(ESEMS.values())
        for e in ENGS:
            own = ESEMS.get(e)
            ws = [(s, snap[s]) for s in esems if s != own and snap.get(s, 0) > 0]
            self.emit(e, None, waits=ws)
        return snap

    def snapshot(self):
        return dict(self.c)


def build_nc():
    nc = bass.Bass("TRN2", target_bir_lowering=False)

    xfull = nc.dram_tensor("xfull", [T, C], F32, kind="ExternalInput").ap()
    xres = nc.dram_tensor("xres", [4096, C], F32, kind="ExternalInput").ap()
    wq = nc.dram_tensor("wq", [C, 512], F32, kind="ExternalInput").ap()
    hhin = nc.dram_tensor("hhin", [1, 512], F32, kind="ExternalInput").ap()
    gnwin = nc.dram_tensor("gnwin", [1, 512], F32, kind="ExternalInput").ap()
    ropec = nc.dram_tensor("ropec", [T, 64], F32, kind="ExternalInput").ap()
    ropes = nc.dram_tensor("ropes", [T, 64], F32, kind="ExternalInput").ap()
    wfcT = nc.dram_tensor("wfcT", [2048, 8192], F32, kind="ExternalInput").ap()
    wpjT = nc.dram_tensor("wpjT", [8192, 2048], F32, kind="ExternalInput").ap()
    identf_d = nc.dram_tensor("identf", [128, 128], F32, kind="ExternalInput").ap()
    identh_d = nc.dram_tensor("identh", [128, 128], BF16, kind="ExternalInput").ap()
    onesh_d = nc.dram_tensor("onesh", [128, 1], BF16, kind="ExternalInput").ap()
    onesrow_d = nc.dram_tensor("onesrow", [1, 128], F32, kind="ExternalInput").ap()
    constv_d = nc.dram_tensor("constv", [128, 8], F32, kind="ExternalInput").ap()
    outd = nc.dram_tensor("out", [4096, C], F32, kind="ExternalOutput").ap()

    xbf_blk = nc.dram_tensor("xbf_blk", [16, T, 128], BF16, kind="Internal").ap()
    xbf_pl = nc.dram_tensor("xbf_pl", [T, 2048], BF16, kind="Internal").ap()
    x2n_blk = nc.dram_tensor("x2n_blk", [64, 512, 128], BF16, kind="Internal").ap()
    scr_fc = nc.dram_tensor("scr_fc", [16, 16, 128, 512], FP8, kind="Internal").ap()
    scr_pj = nc.dram_tensor("scr_pj", [4, 64, 128, 512], FP8, kind="Internal").ap()
    gscr = nc.dram_tensor("gscr", [2, 128], F32, kind="Internal").ap()
    sscr = nc.dram_tensor("sscr", [2, 4], F32, kind="Internal").ap()
    f_dram = nc.dram_tensor("f_dram", [4, 128, 2048], F32, kind="Internal").ap()
    stat_scr = nc.dram_tensor("stat_scr", [2, 512], F32, kind="Internal").ap()
    wcol_dram = nc.dram_tensor("wcol_dram", [4, 512], F32, kind="Internal").ap()

    ctx = ExitStack()
    SB = lambda n, sh, dt: ctx.enter_context(nc.sbuf_tensor(n, sh, dt))
    PS = lambda n, sh, dt: ctx.enter_context(nc.psum_tensor(n, sh, dt))

    garena = SB("garena", [128, 32768], BF16)
    GF = SB("GF", [128, 8192], F32)
    GTx1 = SB("GTx1", [128, 8192], F32)
    XhT = SB("XhT", [128, 8192], BF16)
    wqb = SB("wqb", [128, 8192], BF16)
    xf = [SB("xf0", [128, 2048], F32), SB("xf1", [128, 2048], F32)]
    xresb = SB("xresb", [128, 2048], F32)
    outb = SB("outb", [128, 2048], F32)
    u_bc = SB("u_bc", [128, 512], F32)
    hh_bc = SB("hh_bc", [128, 512], F32)
    tmpu = SB("tmpu", [128, 512], F32)
    tmpr = SB("tmpr", [128, 64], F32)
    rotb = SB("rotb", [128, 64], F32)
    gdump = SB("gdump", [128, 512], BF16)
    ropet_c = [SB("ropetc0", [128, 64], F32), SB("ropetc1", [128, 64], F32)]
    ropet_s = [SB("ropets0", [128, 64], F32), SB("ropets1", [128, 64], F32)]
    stats = SB("stats", [128, 940], F32)
    smalls = SB("smalls", [128, 32], F32)
    gnw_col = SB("gnw_col", [128, 8], F32)
    identf = SB("identfs", [128, 128], F32)
    identh = SB("idenths", [128, 128], BF16)
    onesh = SB("oneshs", [128, 1], BF16)
    onesrow = SB("onesrows", [1, 128], F32)
    constv = SB("constvs", [128, 8], F32)
    for _ci, _cval in enumerate(CONSTVALS):
        nc.const_aps.register(F32, _cval, constv[:, _ci:_ci + 1])
    rowt = SB("rowt", [2, 16], F32)
    srow = SB("srow", [1, 128], F32)
    hh_row = stats[0:1, 428:940]
    u_row = stats[0:1, 428:940]

    PP = [PS(f"pp{i}", [128, 512], F32) for i in range(8)]
    PB16 = [PP[3].bitcast(BF16), PP[5].bitcast(BF16)]

    sem = {s: ctx.enter_context(nc.semaphore(s)) for s in ALLSEMS}

    XT = [garena[:, 0:8192], garena[:, 8192:16384]]
    xb = [garena[:, 16384:18432], garena[:, 18432:20480]]
    xhat_t = garena[:, 20480:22528]
    ropec_sb = garena[:, 22528:24576].bitcast(F32)
    ropes_sb = garena[:, 24576:26624].bitcast(F32)
    wt_m1 = garena[:, 26624:28672]
    wt_out = [garena[:, 28672:30720], garena[:, 30720:32768]]
    fcw = [GF[:, 0:2048].bitcast(FP8), GF[:, 2048:4096].bitcast(FP8)]
    pjw = [XhT[:, 0:2048].bitcast(FP8), XhT[:, 2048:4096].bitcast(FP8)]
    stg_in = [xf[0][:, 0:1024], xf[1][:, 0:1024]]
    stg_out = [xf[0][:, 1024:1280].bitcast(FP8),
               xf[1][:, 1024:1280].bitcast(FP8)]
    stg_m1 = [xf[0][:, 1280:1536].bitcast(FP8),
              xf[1][:, 1280:1536].bitcast(FP8)]
    gam_in = [xresb[:, 0:1024], xresb[:, 1024:2048]]
    xfrow = [xf[0][:, 0:1024].bitcast(BF16), xf[0][:, 1024:2048].bitcast(BF16),
             xf[1][:, 0:1024].bitcast(BF16), xf[1][:, 1024:2048].bitcast(BF16)]
    dumpb = garena[:, 26624:28672]
    dumpf = XhT[:, 4096:6144]

    XS_SUM, XS_SQ = 0, 128
    S_R, S_CS = 256, 272
    S_VS, S_VQ = 288, 352
    S_V4, S_MUV, S_SR = 356, 360, 364
    S_X1S, S_X1Q, S_NM3, S_RHO = 368, 372, 376, 380
    S_GSUM, S_GSQP = 384, 448
    S_GNS, S_GNQ = 512, 640
    S_TMP = 768
    S_G4, S_GQ4, S_NM2, S_RH2 = 896, 900, 904, 908
    S_FS, S_FQ, S_F4, S_NM4, S_R4 = 912, 928, 932, 936, 940
    S_TM2 = 416

    M_CFC, M_GFC, M_NCFC = 0, 1, 2
    M_CPJ, M_GPJ, M_NCPJ = 3, 4, 5
    M_AFC, M_APJ, M_RED, M_GNW2 = 8, 9, 10, 16

    sch = Sch()

    def dma(eng, sm, out, in_, waits=(), slow=False, accum=False):
        kw = {"allow_slow_non_contiguous": True} if slow else {}
        if accum:
            kw["accum_op"] = ALU.add
        sch.emit(eng, (lambda e, o=out, i=in_, kw=kw: e.dma_start(out=o, in_=i,
                                                                  **kw)),
                 inc=sm, waits=waits)
        return sch.now(sm)

    def dmat(eng, sm, out, in_, waits=()):
        sch.emit(eng, (lambda e, o=out, i=in_: e.dma_start(out=o, in_=i,
                                                           transpose=True)),
                 inc=sm, waits=waits)
        return sch.now(sm)

    def op(eng, fn, waits=()):
        sch.emit(eng, fn, inc=ESEMS[eng], waits=waits)
        return sch.now(ESEMS[eng])

    c_id = dma("sync", "dPRO", identf[:], identf_d[:])
    dma("sync", "dPRO", identh[:], identh_d[:])
    c_on = dma("sync", "dPRO", onesh[:], onesh_d[:])
    c_or = dma("sync", "dPRO", onesrow[:], onesrow_d[:])
    dma("sync", "dPRO", constv[:], constv_d[:])
    c_hh = dma("sync", "dPRO", hh_row[:], hhin[:])
    c_gnw = dma("gpsimd", "dGNW", gnw_col[:, 0:4],
                gnwin[0, :].rearrange("(ct p) -> p ct", p=128), slow=True)
    for _kq in range(4):
        c_wq = dma("gpsimd", "dGNW",
                   wqb[:, _kq * 2048:(_kq + 1) * 2048].rearrange(
                       "p (k c) -> p k c", c=512),
                   wq.rearrange("(k p) c -> p k c", p=128)[:, 4 * _kq:4 * _kq + 4, :])

    c_pro = sch.now("dPRO")
    c_gnw = sch.now("dGNW")

    for k in range(16):
        op("tensor", (lambda t, k=k: t.matmul(PP[4][0:1, :], onesh[:],
                                              wqb[:, k * 512:(k + 1) * 512],
                                              start=(k == 0), stop=(k == 15))),
           waits=[("dGNW", c_gnw), ("dPRO", c_pro)])
    cb_hh = op("tensor", lambda t: t.matmul(PP[6][:, :], onesrow[:], hh_row[:],
                                            start=True, stop=True),
               waits=[("dPRO", c_pro)])
    op("vector", lambda v: v.tensor_copy(hh_bc[:], PP[6][:, :]),
       waits=[("sPE", cb_hh)])
    cvu = op("vector", lambda v: v.tensor_copy(u_row[:], PP[4][0:1, :]),
             waits=[("sPE", sch.now("sPE"))])
    cb_u = op("tensor", lambda t: t.matmul(PP[7][:, :], onesrow[:], u_row[:],
                                           start=True, stop=True),
              waits=[("sVE", cvu)])
    op("vector", lambda v: v.tensor_copy(u_bc[:], PP[7][:, :]),
       waits=[("sPE", cb_u)])
    op("vector", lambda v: v.tensor_tensor(
        out=smalls[:, M_GNW2:M_GNW2 + 4], in0=gnw_col[:, 0:4],
        in1=gnw_col[:, 0:4], op=ALU.mult), waits=[("dGNW", c_gnw)])

    chunk_cast = {}
    chunk_xbf = {}
    prev_cast = {}
    xbw = {}
    gam_free = {0: [], 1: []}
    op("vector", lambda v: v.memset(smalls[:, M_AFC:M_AFC + 2], 0.0))
    for g in range(NCH * RT):
        j, rt = divmod(g, 16)
        rows = 128 * g
        wv = []
        if g >= 2:
            wv += [("sVE", prev_cast[g - 2][0]), ("sSE", prev_cast[g - 2][1])]
        cL = dma("sync", "dXF%d" % (g % 2), xf[g % 2][:],
                 xfull[rows:rows + 128, :], waits=wv)
        wv2 = [("dXF%d" % (g % 2), cL)]
        if g >= 2:
            j2, rt2 = divmod(g - 2, 16)
            wv2.append(("dXB%d_%d" % (j2, rt2 % 2), xbw[g - 2]))
        cV = op("vector", (lambda v, g_=g: v.tensor_scalar(
            out=xb[g_ % 2], in0=xf[g_ % 2][:], scalar1=1.0, scalar2=None,
            op0=ALU.mult, op1=ALU.add,
            accum_out=stats[:, XS_SUM + g_:XS_SUM + g_ + 1])), waits=wv2)
        cS2 = op("scalar", (lambda s, g_=g: s.activation(
            out=dumpb, in_=xf[g_ % 2][:], func=AF.Square,
            accum_out=stats[:, XS_SQ + g_:XS_SQ + g_ + 1])),
            waits=[("dXF%d" % (g % 2), cL), ("dPRO", c_pro)])
        prev_cast[g] = (cV, cS2)
        xbw[g] = dma("gpsimd", "dXB%d_%d" % (j, rt % 2),
                     xbf_pl[rows:rows + 128, :], xb[g % 2],
                     waits=[("sVE", cV)])
        if j == 0:
            for kq in range(4):
                xbw[g] = dma(
                    "gpsimd", "dXB%d_%d" % (j, rt % 2),
                    xbf_blk[4 * kq:4 * kq + 4, rows:rows + 128, :].rearrange(
                        "kb r c -> r kb c"),
                    xb[g % 2][:, kq * 512:(kq + 1) * 512].rearrange(
                        "p (kb c) -> p kb c", c=128))
        if rt == RT - 1:
            chunk_cast[j] = (cV, cS2)
            chunk_xbf[j] = (sch.now("dXB%d_0" % j), sch.now("dXB%d_1" % j))
        t, h = divmod(g, 8)
        cgl = dma("scalar", "dSTG0", gam_in[0],
                  wfcT[t * 128:(t + 1) * 128, h * 1024:(h + 1) * 1024],
                  waits=gam_free[0])
        cgr = op("vector", lambda v: v.tensor_reduce(
            smalls[:, M_RED:M_RED + 1], gam_in[0], axis=AX, op=ALU.add,
            apply_absolute_value=True), waits=[("dSTG0", cgl)])
        op("vector", lambda v: v.tensor_tensor(
            smalls[:, M_AFC:M_AFC + 1], smalls[:, M_AFC:M_AFC + 1],
            smalls[:, M_RED:M_RED + 1], op=ALU.add))
        gam_free[0] = [("sVE", cgr)]
        t2, h2 = divmod(g, 2)
        cgl2 = dma("scalar", "dSTG1", gam_in[1],
                   wpjT[t2 * 128:(t2 + 1) * 128, h2 * 1024:(h2 + 1) * 1024],
                   waits=gam_free[1])
        cgr2 = op("vector", lambda v: v.tensor_reduce(
            smalls[:, M_RED + 1:M_RED + 2], gam_in[1], axis=AX, op=ALU.add,
            apply_absolute_value=True), waits=[("dSTG1", cgl2)])
        op("vector", lambda v: v.tensor_tensor(
            smalls[:, M_APJ:M_APJ + 1], smalls[:, M_APJ:M_APJ + 1],
            smalls[:, M_RED + 1:M_RED + 2], op=ALU.add))
        gam_free[1] = [("sVE", cgr2)]
    xv_end = prev_cast[NCH * RT - 1][0]
    xs_end = prev_cast[NCH * RT - 1][1]

    op("vector", lambda v: v.tensor_scalar(
        out=stats[:, XS_SUM:XS_SUM + 128], in0=stats[:, XS_SUM:XS_SUM + 128],
        scalar1=1.0 / 2048, scalar2=None, op0=ALU.mult),
       waits=[("sVE", xv_end), ("sSE", xs_end)])
    op("vector", lambda v: v.tensor_scalar(
        out=stats[:, XS_SQ:XS_SQ + 128], in0=stats[:, XS_SQ:XS_SQ + 128],
        scalar1=1.0 / 2048, scalar2=None, op0=ALU.mult))
    op("vector", lambda v: v.tensor_tensor(
        out=stats[:, S_TMP:S_TMP + 128], in0=stats[:, XS_SUM:XS_SUM + 128],
        in1=stats[:, XS_SUM:XS_SUM + 128], op=ALU.mult))
    cRv = op("vector", lambda v: v.tensor_tensor(
        out=stats[:, XS_SQ:XS_SQ + 128], in0=stats[:, XS_SQ:XS_SQ + 128],
        in1=stats[:, S_TMP:S_TMP + 128], op=ALU.subtract))
    cRl = op("scalar", lambda s: s.activation(
        out=stats[:, S_TMP:S_TMP + 128], in_=stats[:, XS_SQ:XS_SQ + 128],
        func=AF.Ln, bias=1e-5, scale=1.0), waits=[("sVE", cRv), ("dPRO", c_pro)])
    cRe = op("scalar", lambda s: s.activation(
        out=stats[:, XS_SQ:XS_SQ + 128], in_=stats[:, S_TMP:S_TMP + 128],
        func=AF.Exp, bias=0.0, scale=-0.5))
    rstats_ve = op("vector", lambda v: v.tensor_tensor(
        out=stats[:, XS_SUM:XS_SUM + 128], in0=stats[:, XS_SUM:XS_SUM + 128],
        in1=stats[:, XS_SQ:XS_SQ + 128], op=ALU.mult), waits=[("sSE", cRe)])
    rstats_se = cRe

    stg_free = {0: [("sVE", xv_end), ("sSE", xs_end)],
                1: [("sVE", xv_end), ("sSE", xs_end)]}
    scrw = {0: 0, 1: 0, 2: 0, 3: 0}
    m1_free = {0: 0, 1: 0}
    gam_done = {}

    def emit_gamma_pj():
        for u in range(128):
            p = u % 2
            t, h = divmod(u, 2)
            ap = wpjT[t * 128:(t + 1) * 128, h * 1024:(h + 1) * 1024]
            cL = dma("scalar", "dSTG%d" % p, gam_in[p], ap, waits=gam_free[p])
            cR = op("vector", (lambda v, p_=p: v.tensor_reduce(
                smalls[:, M_RED:M_RED + 1], gam_in[p_], axis=AX, op=ALU.add,
                apply_absolute_value=True)), waits=[("dSTG%d" % p, cL)])
            op("vector", lambda v: v.tensor_tensor(
                smalls[:, M_APJ:M_APJ + 1], smalls[:, M_APJ:M_APJ + 1],
                smalls[:, M_RED:M_RED + 1], op=ALU.add))
            gam_free[p] = [("sVE", cR)]

    def emit_gfin(mat):
        idx = 0 if mat == "fc" else 1
        acc = M_AFC + idx
        base = M_CFC if mat == "fc" else M_CPJ
        c1 = dma("sync", "dGAM", gscr[idx, :], smalls[:, acc:acc + 1],
                 waits=[("sVE", sch.now("sVE"))])
        c2 = dma("sync", "dGAM", srow[0:1, 0:128], gscr[idx:idx + 1, :],
                 waits=[("dGAM", c1)])
        op("vector", lambda v: v.reduce_sum(rowt[0:1, 0:1], srow[0:1, 0:128],
                                            axis=AX),
           waits=[("dGAM", c2)])
        op("vector", lambda v: v.tensor_scalar(
            out=rowt[0:1, 1:2], in0=rowt[0:1, 0:1], scalar1=0.5 / NELW,
            scalar2=0.5e-6, op0=ALU.mult, op1=ALU.add))
        gsc = 0.5 / NELW if mat == "fc" else -1.0 / NELW
        op("vector", (lambda v, g_=gsc: v.tensor_scalar(
            out=rowt[0:1, 2:3], in0=rowt[0:1, 0:1], scalar1=g_,
            scalar2=None, op0=ALU.mult)))
        cN = op("vector", lambda v: v.tensor_scalar(
            out=rowt[0:1, 3:4], in0=rowt[0:1, 1:2], scalar1=-1.0,
            scalar2=None, op0=ALU.mult))
        c3 = dma("sync", "dGAM", sscr[idx, 0:3], rowt[0:1, 1:4],
                 waits=[("sVE", cN)])
        c4 = dma("sync", "dGAM", smalls[:, base:base + 3],
                 sscr[idx:idx + 1, 0:3].broadcast_to((128, 3)),
                 waits=[("dGAM", c3)])
        gam_done[mat] = c4

    def emit_tern(mat):
        cgam = sch.now("dGAM")
        for un in range(128):
            p = 0 if mat == "fc" else 1
            scr_sem = "dSCR%d" % (p if mat == "fc" else 2 + p)
            si = p if mat == "fc" else 2 + p
            if mat == "fc":
                ob, kq = divmod(un, 8)
                src3 = wfcT.rearrange("(k p) o -> p k o", p=128)[
                    :, 2 * kq:2 * kq + 2, ob * 512:(ob + 1) * 512]
                dst3 = scr_fc[ob, 2 * kq:2 * kq + 2].rearrange("k p c -> p k c")
            else:
                cbq, u32 = divmod(un, 32)
                src3 = wpjT.rearrange("(o p) c -> p o c", p=128)[
                    :, 2 * u32:2 * u32 + 2, cbq * 512:(cbq + 1) * 512]
                dst3 = scr_pj[cbq, 2 * u32:2 * u32 + 2].rearrange(
                    "o p c -> p o c")
            cL = dma("sync", "dSTG%d" % (2 + p) if False else "dSTG%d" % p,
                     stg_in[p].rearrange("p (k c) -> p k c", c=512), src3,
                     waits=stg_free[p])
            if mat == "fc":
                cSa = op("scalar", (lambda s, p_=p: s.activation(
                    out=stg_m1[p_], in_=stg_in[p_], func=AF.Sign,
                    bias=smalls[:, M_NCFC:M_NCFC + 1], scale=1.0)),
                    waits=[("dSTG%d" % p, cL), ("dGAM", cgam),
                           ("sPO", m1_free[p])])
                cSb = op("scalar", (lambda s, p_=p: s.activation(
                    out=stg_out[p_], in_=stg_in[p_], func=AF.Sign,
                    bias=smalls[:, M_CFC:M_CFC + 1], scale=1.0)),
                    waits=[(scr_sem, scrw[si])])
                cP = op("gpsimd", (lambda g, p_=p: g.tensor_tensor(
                    out=stg_out[p_], in0=stg_out[p_], in1=stg_m1[p_],
                    op=ALU.add)), waits=[("sSE", cSb)])
                stg_free[p] = [("sSE", cSb)]
                m1_free[p] = cP
                wdep = [("sPO", cP)]
            else:
                wm = [("dSTG%d" % p, cL), ("dGAM", cgam),
                      ("sPO", m1_free[p])]
                if un < 2:
                    wm.append(("dSCR%d" % p, sch.now("dSCR%d" % p)))
                cM = op("vector", (lambda v, p_=p: v.tensor_scalar(
                    out=stg_m1[p_], in0=stg_in[p_],
                    scalar1=smalls[:, M_CPJ:M_CPJ + 1], scalar2=None,
                    op0=ALU.is_gt)), waits=wm)
                cS = op("vector", (lambda v, p_=p: v.scalar_tensor_tensor(
                    out=stg_out[p_], in0=stg_in[p_],
                    scalar=smalls[:, M_NCPJ:M_NCPJ + 1], in1=stg_m1[p_],
                    op0=ALU.is_lt, op1=ALU.subtract)),
                    waits=[(scr_sem, scrw[si])])
                stg_free[p] = [("sVE", cS)]
                wdep = [("sVE", cS)]
            cW = dma("scalar", scr_sem, dst3,
                     stg_out[p].rearrange("p (k c) -> p k c", c=512),
                     waits=wdep)
            scrw[si] = cW

    state = {"dOX1": 0, "dOAC": 0, "dump_last": 0, "dump_se": 0, "dF_last": [],
             "pjevac_last": 0, "proj_mm_last": 0, "gn_apply_done": 0,
             "fc_pe_last": 0, "gelu_last": 0, "xres_free": 0, "outb_free_se": 0,
             "outb_free_ve": 0, "x2t_read_done": 0, "stg_handoff": [],
             "xtev": (0, 0), "x1w": {}, "attn_pe_last": 0,
             "xh_done_ve": 0, "xh_done_se": 0, "fcstat_evac": 0,
             "rho_col": 0, "muro_ve": 0, "wcol_pe": 0, "wcol_done": 0}
    rope_free = {0: 0, 1: 0}
    xbws = [("dXB%d_%d" % (jj, p), chunk_xbf[jj][p])
            for jj in range(NCH) for p in range(2)]

    fcw8 = [outb[:].bitcast(FP8), GTx1[:, 4096:6144].bitcast(FP8)]
    pjw8 = [xresb[:, 0:1024].bitcast(FP8), xresb[:, 1024:2048].bitcast(FP8)]
    sqring = [GTx1[:, i * 256:(i + 1) * 256].bitcast(BF16) for i in range(8)]
    ROWR = ([GTx1[0:1, 2048 + i * 512:2560 + i * 512] for i in range(4)]
            + [GTx1[0:1, 6144 + i * 512:6656 + i * 512] for i in range(4)])
    xt_row = [xf[0][:, 0:1024].bitcast(BF16), xf[0][:, 1024:2048].bitcast(BF16)]
    xt_ring = xf[1][:, 0:1024].bitcast(BF16)
    xh_ring = xf[1][:, 1024:2048].bitcast(BF16)
    PBX0 = [PP[0].bitcast(BF16), PP[2].bitcast(BF16)]
    PBX = [PBX0[0][:, 0:128], PBX0[1][:, 0:128]]
    fstg = [GTx1[:, 0:512], GTx1[:, 512:1024]]

    def emit_attn_mm(j):
        bank_free = {}
        chain_done = 0
        xrow_free = {0: [], 1: []}
        cXT = {}
        mm_hist = {}
        rope_cnt = {}
        pbx_ev = {0: ("sVE", 0), 1: ("sVE", 0), 2: ("sVE", 0),
                  3: ("sVE", 0)}

        def emit_chain(rt):
            gsl = GF[:, rt * 512:(rt + 1) * 512]
            wv0 = [("sPE", mm_hist[rt])]
            if rt == 0:
                wv0 += [("sVE", rstats_ve), ("sSE", rstats_se)]
            op("vector", (lambda v, j_=j, rt_=rt: v.tensor_scalar(
                out=tmpu[:], in0=u_bc[:],
                scalar1=stats[:, XS_SUM + 16 * j_ + rt_:
                              XS_SUM + 16 * j_ + rt_ + 1],
                scalar2=None, op0=ALU.mult)), waits=wv0)
            bank_free[rt] = ("sVE", op("vector", (lambda v, j_=j, rt_=rt,
                                                  g_=gsl:
                                                  v.scalar_tensor_tensor(
                out=g_, in0=PP[6 + rt_ % 2][:, :],
                scalar=stats[:, XS_SQ + 16 * j_ + rt_:
                             XS_SQ + 16 * j_ + rt_ + 1], in1=tmpu[:],
                op0=ALU.mult, op1=ALU.subtract))))
            op("vector", (lambda v, g_=gsl: v.tensor_scalar(
                out=rotb[:, 0:64:2], in0=g_[:, 1:64:2], scalar1=-1.0,
                scalar2=None, op0=ALU.mult)),
               waits=[("dROPE", sch.now("dROPE"))])
            op("vector", (lambda v, g_=gsl: v.tensor_scalar(
                out=rotb[:, 1:64:2], in0=g_[:, 0:64:2], scalar1=1.0,
                scalar2=None, op0=ALU.mult)))
            op("vector", (lambda v, g_=gsl, rt_=rt: v.tensor_tensor(
                out=tmpr[:], in0=g_[:, 0:64], in1=ropet_c[rt_ % 2][:],
                op=ALU.mult)))
            op("vector", (lambda v, rt_=rt: v.tensor_tensor(
                out=rotb[:], in0=rotb[:], in1=ropet_s[rt_ % 2][:],
                op=ALU.mult)))
            rope_free[rt % 2] = op("vector", (lambda v, g_=gsl: v.tensor_tensor(
                out=g_[:, 0:64], in0=tmpr[:], in1=rotb[:], op=ALU.add)))
            op("vector", (lambda v, g_=gsl: v.tensor_scalar(
                out=g_, in0=g_, scalar1=0.0, scalar2=None, op0=ALU.max)))
            cHH = op("vector", (lambda v, g_=gsl: v.tensor_tensor(
                out=g_, in0=g_, in1=hh_bc[:], op=ALU.add)))
            cTH = op("scalar", (lambda s, g_=gsl: s.activation(
                out=g_, in_=g_, func=AF.Tanh)), waits=[("sVE", cHH)])
            cSQ = op("scalar", (lambda s, g_=gsl: s.activation(
                out=gdump[:], in_=g_, func=AF.Square)),
                waits=[("sVE", state["dump_last"])])
            op("vector", (lambda v, g_=gsl, rt_=rt: v.tensor_reduce(
                out=stats[:, S_GNS + rt_ * 8:S_GNS + rt_ * 8 + 8],
                in_=g_.rearrange("p (g k) -> p g k", k=64), axis=AX,
                op=ALU.add)), waits=[("sSE", cTH)])
            op("vector", (lambda v, rt_=rt: v.tensor_reduce(
                out=stats[:, S_GNQ + rt_ * 8:S_GNQ + rt_ * 8 + 8],
                in_=gdump[:].rearrange("p (g k) -> p g k", k=64),
                axis=AX, op=ALU.add)), waits=[("sSE", cSQ)])
            b0, q0, t0 = S_GNS + rt * 8, S_GNQ + rt * 8, S_TMP + rt * 8
            op("vector", (lambda v, q_=q0: v.tensor_scalar(
                out=stats[:, q_:q_ + 8], in0=stats[:, q_:q_ + 8],
                scalar1=1.0 / 64, scalar2=None, op0=ALU.mult)))
            op("vector", (lambda v, b_=b0: v.tensor_scalar(
                out=stats[:, b_:b_ + 8], in0=stats[:, b_:b_ + 8],
                scalar1=1.0 / 64, scalar2=None, op0=ALU.mult)))
            op("vector", (lambda v, b_=b0, t_=t0: v.tensor_tensor(
                out=stats[:, t_:t_ + 8], in0=stats[:, b_:b_ + 8],
                in1=stats[:, b_:b_ + 8], op=ALU.mult)))
            cGv = op("vector", (lambda v, q_=q0, t_=t0: v.tensor_tensor(
                out=stats[:, q_:q_ + 8], in0=stats[:, q_:q_ + 8],
                in1=stats[:, t_:t_ + 8], op=ALU.subtract)))
            op("scalar", (lambda s, q_=q0, t_=t0: s.activation(
                out=stats[:, t_:t_ + 8], in_=stats[:, q_:q_ + 8],
                func=AF.Ln, bias=1e-5, scale=1.0)), waits=[("sVE", cGv)])
            cGe = op("scalar", (lambda s, q_=q0, t_=t0: s.activation(
                out=stats[:, q_:q_ + 8], in_=stats[:, t_:t_ + 8],
                func=AF.Exp, bias=0.0, scale=-0.5)))
            op("vector", (lambda v, b_=b0: v.tensor_scalar(
                out=stats[:, b_:b_ + 8], in0=stats[:, b_:b_ + 8],
                scalar1=-1.0, scalar2=None, op0=ALU.mult)),
               waits=[("sSE", cGe)])
            cNm = op("vector", (lambda v, b_=b0, q_=q0: v.tensor_tensor(
                out=stats[:, b_:b_ + 8], in0=stats[:, b_:b_ + 8],
                in1=stats[:, q_:q_ + 8], op=ALU.mult)))
            ap_se = 0
            for g in range(8):
                if g % 2 == 0:
                    op("vector", (lambda v, rt_=rt, g_=g, b_=b0, q_=q0:
                                  v.tensor_scalar(
                        out=GF[:, rt_ * 512 + g_ * 64:rt_ * 512 + g_ * 64 + 64],
                        in0=GF[:, rt_ * 512 + g_ * 64:rt_ * 512 + g_ * 64 + 64],
                        scalar1=stats[:, q_ + g_:q_ + g_ + 1],
                        scalar2=stats[:, b_ + g_:b_ + g_ + 1],
                        op0=ALU.mult, op1=ALU.add)))
                else:
                    ap_se = op("scalar", (lambda s, rt_=rt, g_=g, b_=b0,
                                          q_=q0: s.activation(
                        out=GF[:, rt_ * 512 + g_ * 64:rt_ * 512 + g_ * 64 + 64],
                        in_=GF[:, rt_ * 512 + g_ * 64:rt_ * 512 + g_ * 64 + 64],
                        func=AF.Identity,
                        bias=stats[:, b_ + g_:b_ + g_ + 1],
                        scale=stats[:, q_ + g_:q_ + g_ + 1])),
                        waits=[("sVE", cNm)])
            return sch.now("sVE")

        for rt in range(RT):
            rows = 2048 * j + 128 * rt
            dma("scalar", "dROPE", ropet_c[rt % 2][:], ropec[rows:rows + 128, :],
                waits=[("sVE", rope_free[rt % 2])])
            rope_cnt[rt] = dma("scalar", "dROPE", ropet_s[rt % 2][:],
                               ropes[rows:rows + 128, :])
            if j == 0:
                if rt % 4 == 0:
                    grp = rt // 4
                    wv = [("dXB%d_0" % j, chunk_xbf[j][0]),
                          ("dXB%d_1" % j, chunk_xbf[j][1])]
                    if grp >= 2:
                        wv.append(("sPE", mm_hist[4 * (grp - 2) + 3]))
                    c0 = 0
                    for kb in range(16):
                        c0 = dmat("scalar", "dXT%d" % (grp % 2),
                                  XT[grp % 2][:, kb * 512:(kb + 1) * 512],
                                  xbf_blk[kb, 512 * grp:512 * (grp + 1), :],
                                  waits=wv)
                        wv = []
                    cXT[grp] = c0
                evv = evs = 0
            else:
                slot = rt % 2
                wvr = ([("dXB%d_0" % j, chunk_xbf[j][0]),
                        ("dXB%d_1" % j, chunk_xbf[j][1])]
                       + (state["stg_handoff"] if rt < 2 else [])
                       + ([("sPE", state["attn_pe_last"])] if rt < 2 else [])
                       + xrow_free[slot])
                xsem = "dXTs%d" % slot
                cXR = dma("sync", xsem, xt_row[slot],
                          xbf_pl[rows:rows + 128, :], waits=wvr)
                evv = evs = 0
                for kb in range(16):
                    s2 = kb % 2
                    wvt = [(xsem, cXR), pbx_ev[s2]]
                    if rt == 0 and kb < 2:
                        wvt.append(("sSE", state["gelu_last"]))
                        wvt.append(("sVE", state["pjevac_last"]))
                    cTx = op("tensor", (lambda t, k_=kb, s_=s2, sl_=slot:
                                        t.transpose(
                        PBX[s_],
                        xt_row[sl_][:, k_ * 128:(k_ + 1) * 128],
                        identh[:])), waits=wvt)
                    if kb % 2 == 0:
                        evv = op("vector", (lambda v, k_=kb, s_=s2:
                                            v.tensor_copy(
                            xt_ring[:, k_ * 128:(k_ + 1) * 128],
                            PBX[s_])), waits=[("sPE", cTx)])
                        pbx_ev[s2] = ("sVE", evv)
                    else:
                        evs = op("scalar", (lambda s, k_=kb, s_=s2: s.copy(
                            xt_ring[:, k_ * 128:(k_ + 1) * 128],
                            PBX[s_])), waits=[("sPE", cTx)])
                        pbx_ev[s2] = ("sSE", evs)
                xrow_free[slot] = [("sPE", sch.now("sPE"))]
            if j == 0:
                wvm = [("dXT%d" % ((rt // 4) % 2), cXT[rt // 4])]
                if rt < 2:
                    wvm.append(("sVE", rstats_ve))
            else:
                wvm = [("sVE", evv), ("sSE", evs)]
                if rt < 2:
                    wvm.append(("sVE", state["fcstat_evac"]))
            if rt >= 2:
                wvm.append(bank_free[rt - 2])
            mm_last = 0
            for kb in range(16):
                if j == 0:
                    lhs = XT[(rt // 4) % 2][:, kb * 512 + (rt % 4) * 128:
                                            kb * 512 + (rt % 4) * 128 + 128]
                else:
                    lhs = xt_ring[:, kb * 128:(kb + 1) * 128]
                mm_last = op("tensor", (lambda t, l_=lhs, k=kb, rt_=rt:
                                        t.matmul(
                    PP[6 + rt_ % 2][:, :], l_,
                    wqb[:, k * 512:(k + 1) * 512],
                    start=(k == 0), stop=(k == 15))), waits=wvm)
                wvm = []
            mm_hist[rt] = mm_last
            if rt >= 1:
                chain_done = emit_chain(rt - 1)
        chain_done = emit_chain(RT - 1)
        state["attn_pe_last"] = sch.now("sPE")
        state["xtev"] = (sch.now("sVE"), sch.now("sSE"))
        state["dump_se"] = sch.now("sSE")
        return chain_done

    def emit_attn_tail(j, chain_done):
        gt_slot = {0: ("sVE", 0), 1: ("sVE", 0), 2: ("sVE", 0), 3: ("sVE", 0)}
        for ct in range(4):
            for rtb in range(RT):
                i = ct * RT + rtb
                slot4 = i % 4
                bank4 = [1, 3, 4, 5][slot4]
                wv = [("sVE", state["pjevac_last"]), gt_slot[slot4]] + \
                     (state["dF_last"] if i < 2 else []) + \
                     ([("sVE", chain_done),
                       ("sSE", state["dump_se"])] if i < 4 else [])
                cTt = op("tensor", (lambda t, c_=ct, r_=rtb, b_=bank4:
                                    t.transpose(
                    PP[b_][:, 0:128],
                    GF[:, r_ * 512 + c_ * 128:r_ * 512 + c_ * 128 + 128],
                    identf[:])), waits=wv)
                if i % 2 == 0:
                    c = op("vector", (lambda v, c_=ct, r_=rtb, b_=bank4:
                                      v.tensor_scalar(
                        out=GTx1[:, c_ * 2048 + r_ * 128:
                                 c_ * 2048 + r_ * 128 + 128],
                        in0=PP[b_][:, 0:128],
                        scalar1=1.0, scalar2=None, op0=ALU.mult, op1=ALU.add,
                        accum_out=stats[:, S_VS + c_ * 16 + r_:
                                        S_VS + c_ * 16 + r_ + 1])),
                        waits=[("sPE", cTt)])
                    gt_slot[slot4] = ("sVE", c)
                else:
                    c = op("scalar", (lambda s, c_=ct, r_=rtb, b_=bank4:
                                      s.activation(
                        out=GTx1[:, c_ * 2048 + r_ * 128:
                                 c_ * 2048 + r_ * 128 + 128],
                        in_=PP[b_][:, 0:128], func=AF.Identity,
                        accum_out=stats[:, S_VS + c_ * 16 + r_:
                                        S_VS + c_ * 16 + r_ + 1])),
                        waits=[("sPE", cTt)])
                    gt_slot[slot4] = ("sSE", c)
        vsq_done = 0
        for ct in range(4):
            for qq in range(4):
                vsq_done = op("vector", (lambda v, c_=ct, q_=qq:
                                         v.scalar_tensor_tensor(
                    out=gdump[:],
                    in0=GTx1[:, c_ * 2048 + q_ * 512:c_ * 2048 + q_ * 512 + 512],
                    scalar=1.0,
                    in1=GTx1[:, c_ * 2048 + q_ * 512:c_ * 2048 + q_ * 512 + 512],
                    op0=ALU.mult, op1=ALU.mult,
                    accum_out=stats[:, S_GSUM + c_ * 4 + q_:
                                    S_GSUM + c_ * 4 + q_ + 1])),
                    waits=([("sSE", sch.now("sSE"))] if ct == 0 and qq == 0
                           else []))
        op("vector", lambda v: v.tensor_reduce(
            out=stats[:, S_VQ:S_VQ + 4],
            in_=stats[:, S_GSUM:S_GSUM + 16].rearrange("p (a b) -> p a b", b=4),
            axis=AX, op=ALU.add), waits=[("sVE", vsq_done)])
        op("vector", lambda v: v.tensor_reduce(
            out=stats[:, S_V4:S_V4 + 4],
            in_=stats[:, S_VS:S_VS + 64].rearrange("p (a b) -> p a b", b=16),
            axis=AX, op=ALU.add))
        op("vector", lambda v: v.tensor_scalar(
            out=stats[:, S_MUV:S_MUV + 4], in0=stats[:, S_V4:S_V4 + 4],
            scalar1=1.0 / 2048, scalar2=None, op0=ALU.mult))
        op("vector", lambda v: v.tensor_scalar(
            out=stats[:, S_TMP:S_TMP + 4], in0=stats[:, S_VQ:S_VQ + 4],
            scalar1=1.0 / 2048, scalar2=None, op0=ALU.mult))
        op("vector", lambda v: v.tensor_tensor(
            out=stats[:, S_TMP + 4:S_TMP + 8], in0=stats[:, S_MUV:S_MUV + 4],
            in1=stats[:, S_MUV:S_MUV + 4], op=ALU.mult))
        op("vector", lambda v: v.tensor_tensor(
            out=stats[:, S_TMP:S_TMP + 4], in0=stats[:, S_TMP:S_TMP + 4],
            in1=stats[:, S_TMP + 4:S_TMP + 8], op=ALU.subtract))
        cVg = op("vector", lambda v: v.tensor_tensor(
            out=stats[:, S_TMP:S_TMP + 4], in0=stats[:, S_TMP:S_TMP + 4],
            in1=smalls[:, M_GNW2:M_GNW2 + 4], op=ALU.mult))
        cLv = op("scalar", lambda s: s.activation(
            out=stats[:, S_TMP + 4:S_TMP + 8], in_=stats[:, S_TMP:S_TMP + 4],
            func=AF.Ln, bias=1e-5, scale=1.0), waits=[("sVE", cVg)])
        cEv2 = op("scalar", lambda s: s.activation(
            out=stats[:, S_TMP:S_TMP + 4], in_=stats[:, S_TMP + 4:S_TMP + 8],
            func=AF.Exp, bias=0.0, scale=-0.5))
        cSr = op("vector", lambda v: v.tensor_tensor(
            out=stats[:, S_SR:S_SR + 4], in0=gnw_col[:, 0:4],
            in1=stats[:, S_TMP:S_TMP + 4], op=ALU.mult), waits=[("sSE", cEv2)])
        cNmv = op("vector", lambda v: v.tensor_scalar(
            out=stats[:, S_MUV:S_MUV + 4], in0=stats[:, S_MUV:S_MUV + 4],
            scalar1=-1.0, scalar2=None, op0=ALU.mult))
        x1_done = {}
        for ct in range(4):
            r0 = j + 1024 * ct
            cXr = dma("scalar", "dXRES", xresb[:], xres[r0:r0 + 1017:8, :],
                      waits=([("sVE", x1_done.get(ct - 1, 0)),
                              ("sPE", state["proj_mm_last"])]
                             + (gam_free[0] + gam_free[1]
                                if ct == 0 and j == 0 else [])))
            cA1 = op("vector", (lambda v, c_=ct: v.tensor_scalar(
                out=GTx1[:, c_ * 2048:(c_ + 1) * 2048],
                in0=GTx1[:, c_ * 2048:(c_ + 1) * 2048],
                scalar1=stats[:, S_MUV + c_:S_MUV + c_ + 1],
                scalar2=stats[:, S_SR + c_:S_SR + c_ + 1],
                op0=ALU.add, op1=ALU.mult)),
                waits=[("sVE", cSr), ("sVE", cNmv)])
            cA2 = op("vector", (lambda v, c_=ct: v.scalar_tensor_tensor(
                out=GTx1[:, c_ * 2048:(c_ + 1) * 2048],
                in0=GTx1[:, c_ * 2048:(c_ + 1) * 2048], scalar=1.0,
                in1=xresb[:], op0=ALU.mult, op1=ALU.add,
                accum_out=stats[:, S_X1S + c_:S_X1S + c_ + 1])),
                waits=[("dXRES", cXr)])
            state["x1w"][(j, ct)] = dma(
                "scalar", "dOX1", outd[r0:r0 + 1017:8, :],
                GTx1[:, ct * 2048:(ct + 1) * 2048], waits=[("sVE", cA2)])
            x1_done[ct] = op("vector", (lambda v, c_=ct:
                                        v.scalar_tensor_tensor(
                out=gdump[:],
                in0=GTx1[:, c_ * 2048:c_ * 2048 + 512], scalar=1.0,
                in1=GTx1[:, c_ * 2048:c_ * 2048 + 512], op0=ALU.mult,
                op1=ALU.mult,
                accum_out=stats[:, S_GSUM + c_:S_GSUM + c_ + 1])))
            for qq in range(1, 4):
                x1_done[ct] = op("vector", (lambda v, c_=ct, q_=qq:
                                            v.scalar_tensor_tensor(
                    out=gdump[:],
                    in0=GTx1[:, c_ * 2048 + q_ * 512:
                             c_ * 2048 + q_ * 512 + 512], scalar=1.0,
                    in1=GTx1[:, c_ * 2048 + q_ * 512:
                             c_ * 2048 + q_ * 512 + 512], op0=ALU.mult,
                    op1=ALU.mult,
                    accum_out=stats[:, S_GSUM + 16 + c_ * 3 + q_ - 1:
                                    S_GSUM + 16 + c_ * 3 + q_])))
        state["xres_free"] = x1_done[3]
        state["dump_last"] = x1_done[3]
        op("vector", lambda v: v.tensor_reduce(
            out=stats[:, S_X1Q:S_X1Q + 4],
            in_=stats[:, S_GSUM + 16:S_GSUM + 16 + 12].rearrange(
                "p (a b) -> p a b", b=3),
            axis=AX, op=ALU.add), waits=[("sVE", x1_done[3])])
        op("vector", lambda v: v.tensor_tensor(
            out=stats[:, S_X1Q:S_X1Q + 4], in0=stats[:, S_X1Q:S_X1Q + 4],
            in1=stats[:, S_GSUM:S_GSUM + 4], op=ALU.add))
        op("vector", lambda v: v.tensor_scalar(
            out=stats[:, S_TMP:S_TMP + 4], in0=stats[:, S_X1S:S_X1S + 4],
            scalar1=1.0 / 2048, scalar2=None, op0=ALU.mult))
        op("vector", lambda v: v.tensor_scalar(
            out=stats[:, S_TMP + 4:S_TMP + 8], in0=stats[:, S_X1Q:S_X1Q + 4],
            scalar1=1.0 / 2048, scalar2=None, op0=ALU.mult))
        op("vector", lambda v: v.tensor_tensor(
            out=stats[:, S_TMP + 8:S_TMP + 12], in0=stats[:, S_TMP:S_TMP + 4],
            in1=stats[:, S_TMP:S_TMP + 4], op=ALU.mult))
        cV3 = op("vector", lambda v: v.tensor_tensor(
            out=stats[:, S_TMP + 4:S_TMP + 8], in0=stats[:, S_TMP + 4:S_TMP + 8],
            in1=stats[:, S_TMP + 8:S_TMP + 12], op=ALU.subtract))
        cL3 = op("scalar", lambda s: s.activation(
            out=stats[:, S_TMP + 8:S_TMP + 12],
            in_=stats[:, S_TMP + 4:S_TMP + 8],
            func=AF.Ln, bias=1e-5, scale=1.0), waits=[("sVE", cV3)])
        cE3 = op("scalar", lambda s: s.activation(
            out=stats[:, S_TMP + 12:S_TMP + 16],
            in_=stats[:, S_TMP + 8:S_TMP + 12],
            func=AF.Exp, bias=0.0, scale=-0.5))
        op("vector", lambda v: v.tensor_tensor(
            out=stats[:, S_TMP + 16:S_TMP + 20],
            in0=stats[:, S_TMP + 12:S_TMP + 16],
            in1=stats[:, S_TMP + 12:S_TMP + 16], op=ALU.mult),
           waits=[("sSE", cE3)])
        cV4 = op("vector", lambda v: v.tensor_tensor(
            out=stats[:, S_TMP + 16:S_TMP + 20],
            in0=stats[:, S_TMP + 4:S_TMP + 8],
            in1=stats[:, S_TMP + 16:S_TMP + 20], op=ALU.mult))
        cL4 = op("scalar", lambda s: s.activation(
            out=stats[:, S_TMP + 20:S_TMP + 24],
            in_=stats[:, S_TMP + 16:S_TMP + 20],
            func=AF.Ln, bias=1e-6, scale=1.0), waits=[("sVE", cV4)])
        cE4 = op("scalar", lambda s: s.activation(
            out=stats[:, S_TMP + 20:S_TMP + 24],
            in_=stats[:, S_TMP + 20:S_TMP + 24],
            func=AF.Exp, bias=0.0, scale=-0.5))
        op("vector", lambda v: v.tensor_tensor(
            out=stats[:, S_RHO:S_RHO + 4], in0=stats[:, S_TMP + 12:S_TMP + 16],
            in1=stats[:, S_TMP + 20:S_TMP + 24], op=ALU.mult),
           waits=[("sSE", cE4)])
        cRho = op("vector", lambda v: v.tensor_scalar(
            out=stats[:, S_RHO:S_RHO + 4], in0=stats[:, S_RHO:S_RHO + 4],
            scalar1=smalls[:, M_GFC:M_GFC + 1], scalar2=None, op0=ALU.mult),
            waits=[("dGAM", gam_done["fc"])])
        cNm3 = op("vector", lambda v: v.tensor_scalar(
            out=stats[:, S_NM3:S_NM3 + 4], in0=stats[:, S_TMP:S_TMP + 4],
            scalar1=-1.0, scalar2=None, op0=ALU.mult))
        xh_done = sch.now("sPE")
        xhat_buf = xhat_t if j == 0 else xh_ring
        xh_slot = {0: ("sVE", 0), 1: ("sSE", 0)}
        for ct in range(4):
            cXh = op("vector", (lambda v, c_=ct, b_=xhat_buf: v.tensor_scalar(
                out=b_, in0=GTx1[:, c_ * 2048:(c_ + 1) * 2048],
                scalar1=stats[:, S_NM3 + c_:S_NM3 + c_ + 1],
                scalar2=stats[:, S_RHO + c_:S_RHO + c_ + 1],
                op0=ALU.add, op1=ALU.mult)),
                waits=[("sVE", cRho), ("sVE", cNm3), ("sPE", xh_done)])
            for k in range(16):
                s2 = k % 2
                cTp = op("tensor", (lambda t, k_=k, s_=s2, b_=xhat_buf:
                                    t.transpose(
                    PBX[s_][:, 0:128],
                    b_[:, k_ * 128:(k_ + 1) * 128], identh[:])),
                    waits=[("sVE", cXh), xh_slot[s2]])
                if k % 2 == 0:
                    c = op("vector", (lambda v, k_=k, c_=ct, s_=s2:
                                      v.tensor_copy(
                        XhT[:, k_ * 512 + c_ * 128:k_ * 512 + c_ * 128 + 128],
                        PBX[s_][:, 0:128])), waits=[("sPE", cTp)])
                    xh_slot[s2] = ("sVE", c)
                else:
                    c = op("scalar", (lambda s, k_=k, c_=ct, s_=s2: s.copy(
                        XhT[:, k_ * 512 + c_ * 128:k_ * 512 + c_ * 128 + 128],
                        PBX[s_][:, 0:128])), waits=[("sPE", cTp)])
                    xh_slot[s2] = ("sSE", c)
            xh_done = sch.now("sPE")
        state["xh_done_ve"] = sch.now("sVE")
        state["xh_done_se"] = sch.now("sSE")

    def emit_ffn_fc(j):
        gelu_done = {}
        sq_stt = {}
        sq_mm = {}
        fcw_free = {0: [("dOAC", state["dOAC"]), ("sVE", state["outb_free_ve"]),
                        ("sSE", state["outb_free_se"])],
                    1: [("sVE", state["xh_done_ve"]),
                        ("sSE", state["xh_done_se"]),
                        ("dOX1", sch.now("dOX1"))]}
        mm = 0
        mm_chb = {}

        def emit_stats(chb):
            sq_mm[chb] = op("tensor", (lambda t, c_=chb: t.matmul(
                PP[6][0:1, :], onesh[:], garena[:, c_ * 512:(c_ + 1) * 512],
                start=(c_ == 0), stop=(c_ == 63))),
                waits=[("sSE", gelu_done[chb])] +
                      ([("sVE", state["xtev"][0])] if chb == 0 else []))
            sq_mm[chb] = op("tensor", (lambda t, c_=chb: t.matmul(
                PP[6][32:33, :], onesh[:], sqring[c_ % 8][:],
                start=(c_ == 0), stop=(c_ == 63))),
                waits=[("sVE", sq_stt[chb])])

        for ob in range(16):
            if ob >= 2:
                for cq in range(4):
                    emit_stats(4 * (ob - 2) + cq)
            cWl = 0
            for kq in range(4):
                cWl = dma("sync", "dWF%d" % (ob % 2),
                          fcw8[ob % 2][:, kq * 2048:(kq + 1) * 2048].rearrange(
                              "p (k c) -> p k c", c=512),
                          scr_fc[ob, 4 * kq:4 * kq + 4].rearrange(
                              "k p c -> p k c"),
                          waits=(fcw_free[ob % 2]
                                 + [("dSCR0", sch.now("dSCR0")),
                                    ("dSCR1", sch.now("dSCR1"))]
                                 if kq == 0 else []))
            for k in range(16):
                for cq in range(4):
                    chb = 4 * ob + cq
                    bank = chb % 6
                    wv = [("dWF%d" % (ob % 2), cWl),
                          ("sVE", state["xh_done_ve"]),
                          ("sSE", state["xh_done_se"])]
                    if chb >= 6 and k == 0:
                        wv.append(("sSE", gelu_done[chb - 6]))
                    mm = op("tensor", (lambda t, k_=k, q_=cq, o_=ob, b_=bank:
                                       t.matmul(
                        PP[b_][:, :],
                        fcw8[o_ % 2][:, k_ * 512 + q_ * 128:
                                     k_ * 512 + q_ * 128 + 128],
                        XhT[:, k_ * 512:(k_ + 1) * 512],
                        start=(k_ == 0), stop=(k_ == 15))), waits=wv)
                    if k == 15:
                        mm_chb[chb] = mm
            fcw_free[ob % 2] = [("sPE", mm)]
            for cq in range(4):
                chb = 4 * ob + cq
                bank = chb % 6
                gelu_done[chb] = op("scalar", (lambda s, c_=chb, b_=bank:
                                              s.activation(
                    out=garena[:, c_ * 512:(c_ + 1) * 512],
                    in_=PP[b_][:, :], func=AF.Gelu)),
                    waits=[("sPE", mm_chb[chb])] +
                          (xbws if chb == 0 and j == 0 else []))
                wsq = [("sSE", gelu_done[chb])]
                if chb < 8:
                    wsq += state["dF_last"]
                    wsq += [("dOX1", sch.now("dOX1")),
                            ("sPE", state["fc_pe_last"])]
                else:
                    wsq += [("sPE", sq_mm[chb - 8])]
                sq_stt[chb] = op("vector", (lambda v, c_=chb:
                                            v.scalar_tensor_tensor(
                    out=sqring[c_ % 8][:],
                    in0=garena[:, c_ * 512:(c_ + 1) * 512], scalar=1.0,
                    in1=garena[:, c_ * 512:(c_ + 1) * 512],
                    op0=ALU.mult, op1=ALU.mult)), waits=wsq)
        for cq in range(4):
            emit_stats(56 + cq)
        for cq in range(4):
            emit_stats(60 + cq)
        state["gelu_last"] = gelu_done[63]
        state["fc_pe_last"] = sch.now("sPE")
        return gelu_done

    def emit_wcol():
        wfree = {0: [("sVE", state["xres_free"])],
                 1: [("sVE", state["xres_free"])]}
        prevW = 0
        for cb in range(4):
            mm = 0
            for og in range(8):
                cPl = dma("sync", "dWP%d" % (og % 2),
                          pjw8[og % 2][:].rearrange("p (o c) -> p o c", c=512),
                          scr_pj[cb, 8 * og:8 * og + 8].rearrange(
                              "o p c -> p o c"),
                          waits=(wfree[og % 2]
                                 + [("dSCR2", sch.now("dSCR2")),
                                    ("dSCR3", sch.now("dSCR3"))]))
                for o in range(8):
                    wrow = [PP[7][0:1, :], PP[7][32:33, :], PP[7][64:65, :],
                            PP[6][64:65, :]][cb]
                    mm = op("tensor", (lambda t, w_=wrow, o_=o, s_=og % 2,
                                       st_=(og == 0 and o == 0),
                                       sp_=(og == 7 and o == 7): t.matmul(
                        w_, onesh[:],
                        pjw8[s_][:, o_ * 512:(o_ + 1) * 512],
                        start=st_, stop=sp_)),
                        waits=[("dWP%d" % (og % 2), cPl)] +
                              ([("sVE", state["xtev"][0])]
                               if cb == 0 and og == 0 else []))
                wfree[og % 2] = [("sPE", mm)]
            wrow = [PP[7][0:1, :], PP[7][32:33, :], PP[7][64:65, :],
                    PP[6][64:65, :]][cb]
            cCp = op("vector", (lambda v, w_=wrow: v.tensor_copy(
                ROWR[6][:], w_)),
                waits=[("sPE", mm), ("dSTAT", prevW)])
            prevW = dma("sync", "dSTAT", wcol_dram[cb:cb + 1, :],
                        ROWR[6][:], waits=[("sVE", cCp)])
        state["wcol_done"] = prevW
        state["wcol_pe"] = sch.now("sPE")

    def emit_ffn_mid(j):
        w0 = [("sPE", state["fc_pe_last"])]
        op("vector", lambda v: v.tensor_scalar(
            out=ROWR[2][:], in0=PP[6][0:1, :], scalar1=1.0 / 8192,
            scalar2=None, op0=ALU.mult), waits=w0)
        cEq = op("vector", lambda v: v.tensor_scalar(
            out=ROWR[3][:], in0=PP[6][32:33, :], scalar1=1.0 / 8192,
            scalar2=None, op0=ALU.mult))
        op("vector", lambda v: v.tensor_tensor(
            out=ROWR[5][:], in0=ROWR[2][:], in1=ROWR[2][:],
            op=ALU.mult))
        cVr = op("vector", lambda v: v.tensor_tensor(
            out=ROWR[3][:], in0=ROWR[3][:], in1=ROWR[5][:],
            op=ALU.subtract))
        cLn = op("scalar", lambda s: s.activation(
            out=ROWR[5][:], in_=ROWR[3][:], func=AF.Ln, bias=1e-6,
            scale=1.0), waits=[("sVE", cVr)])
        cEx = op("scalar", lambda s: s.activation(
            out=ROWR[3][:], in_=ROWR[5][:], func=AF.Exp, bias=0.0,
            scale=-0.5))
        cRh = op("vector", lambda v: v.tensor_scalar(
            out=ROWR[3][:], in0=ROWR[3][:],
            scalar1=smalls[0:1, M_GPJ:M_GPJ + 1], scalar2=None, op0=ALU.mult),
            waits=[("sSE", cEx), ("dGAM", gam_done["pj"])])
        cMuro = op("vector", lambda v: v.tensor_scalar(
            out=ROWR[4][:], in0=ROWR[2][:], scalar1=-1.0, scalar2=None,
            op0=ALU.mult))
        c1 = dma("sync", "dSTAT", stat_scr[0:1, :], ROWR[3][:],
                 waits=[("sVE", cRh)])
        c2 = dma("sync", "dSTAT", stats[:, S_RH2:S_RH2 + 4],
                 stat_scr[0].rearrange("(s p) -> p s", p=128),
                 waits=[("dSTAT", c1)], slow=True)
        state["rho_col"] = c2
        state["muro_ve"] = cMuro
        state["fcstat_evac"] = cMuro

    def emit_ffn_proj(j, gelu_done):
        pj_evac = {}
        pjw_free = {0: [("sVE", state["xres_free"]),
                        ("sPE", state["wcol_pe"])],
                    1: [("sVE", state["xres_free"]),
                        ("sPE", state["wcol_pe"])]}
        fst_free = {0: ("dOX1", sch.now("dOX1")),
                    1: ("dOX1", sch.now("dOX1"))}
        wsl = {}
        rank1 = {}
        for cb in range(4):
            wsl[cb] = dma("scalar", "dWCL" if cb % 2 == 0 else "dWCL1",
                          ROWR[6 + cb % 2][:],
                          wcol_dram[cb:cb + 1, :],
                          waits=([("sPE", rank1.get(cb - 2, 0))]
                                 + ([("dSTAT", sch.now("dSTAT"))]
                                    if cb < 2 else [])))
            mm = 0
            for og in range(8):
                cPl = dma("sync", "dWP%d" % (og % 2),
                          pjw8[og % 2][:].rearrange("p (o c) -> p o c", c=512),
                          scr_pj[cb, 8 * og:8 * og + 8].rearrange(
                              "o p c -> p o c"),
                          waits=(pjw_free[og % 2]
                                 + [("dSCR2", sch.now("dSCR2")),
                                    ("dSCR3", sch.now("dSCR3"))]))
                for o in range(8 * og, 8 * og + 8):
                    for st in range(4):
                        bank = (4 * cb + st) % 6
                        wv = [("dWP%d" % (og % 2), cPl),
                              ("sSE", gelu_done[o]),
                              ("sVE", state["xtev"][0]),
                              ("sSE", state["xtev"][1])]
                        if 4 * cb + st >= 6 and o == 0:
                            cb2, st2 = divmod(4 * cb + st - 6, 4)
                            wv.append(("sVE", pj_evac[(cb2, st2)]))
                        mm = op("tensor", (lambda t, o_=o, s_=st, b_=bank:
                                           t.matmul(
                            PP[b_][:, :],
                            garena[:, o_ * 512 + s_ * 128:
                                   o_ * 512 + s_ * 128 + 128],
                            pjw8[(o_ // 8) % 2][:, (o_ % 8) * 512:
                                                (o_ % 8) * 512 + 512],
                            start=(o_ == 0), stop=False)), waits=wv)
                pjw_free[og % 2] = [("sPE", mm)]
            for st in range(4):
                bank = (4 * cb + st) % 6
                mm = op("tensor", (lambda t, s_=st, b_=bank, c_=cb: t.matmul(
                    PP[b_][:, :], ROWR[4][:, s_ * 128:(s_ + 1) * 128],
                    ROWR[6 + c_ % 2][:],
                    start=False, stop=True)),
                    waits=[("sVE", state["muro_ve"]),
                           ("dWCL" if cb % 2 == 0 else "dWCL1", wsl[cb])])
                rank1[cb] = mm
            for st in range(4):
                bank = (4 * cb + st) % 6
                slot = (cb * 4 + st) % 2
                pj_evac[(cb, st)] = op("vector", (lambda v, s_=st, b_=bank,
                                                  sl_=slot, c_=cb:
                                                  v.tensor_scalar(
                    out=fstg[sl_], in0=PP[b_][:, :],
                    scalar1=stats[:, S_RH2 + s_:S_RH2 + s_ + 1], scalar2=None,
                    op0=ALU.mult, op1=ALU.add,
                    accum_out=stats[:, S_FS + s_ * 4 + c_:
                                    S_FS + s_ * 4 + c_ + 1])),
                    waits=[("sPE", rank1[cb]), fst_free[slot],
                           ("dSTAT", state["rho_col"])])
                cw = dma("gpsimd", "dF%d" % slot,
                         f_dram[st, :, cb * 512:(cb + 1) * 512],
                         fstg[slot], waits=[("sVE", pj_evac[(cb, st)])])
                fst_free[slot] = ("dF%d" % slot, cw)
        state["proj_mm_last"] = sch.now("sPE")
        state["pjevac_last"] = sch.now("sVE")
        state["dF_last"] = [("dF0", sch.now("dF0")), ("dF1", sch.now("dF1"))]
        return state["dF_last"]

    def emit_ffn_out(j, f_written):
        op("vector", lambda v: v.tensor_reduce(
            out=stats[:, S_F4:S_F4 + 4],
            in_=stats[:, S_FS:S_FS + 16].rearrange("p (a b) -> p a b", b=4),
            axis=AX, op=ALU.add), waits=[("sVE", state["pjevac_last"])])
        ac = 0
        prev_consumed = 0
        for st in range(4):
            w = list(f_written)
            if st == 0:
                w += [("dOAC", state["dOAC"]), ("sVE", state["outb_free_ve"]),
                      ("sSE", state["outb_free_se"])]
            else:
                w += [("sVE", prev_consumed), ("dOAC", ac)]
            cRd = dma("sync", "dFR", outb[:], f_dram[st], waits=w)
            fq = 0
            for qq in range(4):
                wq_ = [("dFR", cRd)]
                if st == 0 and qq == 0:
                    wq_ += [("sVE", state["dump_last"]),
                            ("sSE", state["dump_se"])]
                fq = op("vector", (lambda v, q_=qq, s_=st:
                                   v.scalar_tensor_tensor(
                    out=gdump[:], in0=outb[:, q_ * 512:(q_ + 1) * 512],
                    scalar=1.0, in1=outb[:, q_ * 512:(q_ + 1) * 512],
                    op0=ALU.mult, op1=ALU.mult,
                    accum_out=stats[:, 384 + s_ * 4 + q_:
                                    384 + s_ * 4 + q_ + 1])),
                    waits=wq_)
            state["dump_last"] = fq
            op("vector", (lambda v, s_=st: v.tensor_reduce(
                out=stats[:, S_TM2 + 44 + s_:S_TM2 + 44 + s_ + 1],
                in_=stats[:, 384 + s_ * 4:384 + s_ * 4 + 4].rearrange(
                    "p (a b) -> p a b", b=4),
                axis=AX, op=ALU.add)), waits=[("sVE", fq)])
            op("vector", (lambda v, s_=st: v.tensor_scalar(
                out=stats[:, S_TM2 + 48 + s_:S_TM2 + 48 + s_ + 1],
                in0=stats[:, S_TM2 + 44 + s_:S_TM2 + 44 + s_ + 1],
                scalar1=1.0 / 2048, scalar2=None, op0=ALU.mult)))
            op("vector", (lambda v, s_=st: v.tensor_scalar(
                out=stats[:, S_TM2 + 52 + s_:S_TM2 + 52 + s_ + 1],
                in0=stats[:, S_F4 + s_:S_F4 + s_ + 1],
                scalar1=1.0 / 2048, scalar2=None, op0=ALU.mult)))
            op("vector", (lambda v, s_=st: v.tensor_tensor(
                out=stats[:, S_TM2 + 56 + s_:S_TM2 + 56 + s_ + 1],
                in0=stats[:, S_TM2 + 52 + s_:S_TM2 + 52 + s_ + 1],
                in1=stats[:, S_TM2 + 52 + s_:S_TM2 + 52 + s_ + 1],
                op=ALU.mult)))
            cVf = op("vector", (lambda v, s_=st: v.tensor_tensor(
                out=stats[:, S_TM2 + 48 + s_:S_TM2 + 48 + s_ + 1],
                in0=stats[:, S_TM2 + 48 + s_:S_TM2 + 48 + s_ + 1],
                in1=stats[:, S_TM2 + 56 + s_:S_TM2 + 56 + s_ + 1],
                op=ALU.subtract)))
            cLf = op("scalar", (lambda s, s_=st: s.activation(
                out=stats[:, S_TM2 + 56 + s_:S_TM2 + 56 + s_ + 1],
                in_=stats[:, S_TM2 + 48 + s_:S_TM2 + 48 + s_ + 1],
                func=AF.Ln, bias=1e-5, scale=1.0)), waits=[("sVE", cVf)])
            cEf = op("scalar", (lambda s, s_=st: s.activation(
                out=stats[:, S_TM2 + 60 + s_:S_TM2 + 60 + s_ + 1],
                in_=stats[:, S_TM2 + 56 + s_:S_TM2 + 56 + s_ + 1],
                func=AF.Exp, bias=0.0, scale=-0.5)))
            cNm4 = op("vector", (lambda v, s_=st: v.tensor_scalar(
                out=stats[:, S_TM2 + 64 + s_:S_TM2 + 64 + s_ + 1],
                in0=stats[:, S_TM2 + 52 + s_:S_TM2 + 52 + s_ + 1],
                scalar1=-1.0, scalar2=None, op0=ALU.mult)),
                waits=[("sSE", cEf)])
            cO1 = op("vector", (lambda v, s_=st: v.tensor_scalar(
                out=outb[:], in0=outb[:],
                scalar1=stats[:, S_TM2 + 64 + s_:S_TM2 + 64 + s_ + 1],
                scalar2=stats[:, S_TM2 + 60 + s_:S_TM2 + 60 + s_ + 1],
                op0=ALU.add, op1=ALU.mult)))
            prev_consumed = cO1
            r0 = j + 1024 * st
            ac = dma("gpsimd", "dOAC", outd[r0:r0 + 1017:8, :], outb[:],
                     waits=[("sVE", cO1), ("dOX1", sch.now("dOX1"))],
                     accum=True)
            state["outb_free_ve"] = cO1
        state["dOAC"] = ac

    emit_gfin("fc")
    emit_gfin("pj")
    cd0 = emit_attn_mm(0)
    emit_tern("fc")
    emit_attn_tail(0, cd0)
    emit_tern("pj")
    state["stg_handoff"] = [("sSE", sch.now("sSE")), ("sVE", sch.now("sVE")),
                            ("dSCR0", sch.now("dSCR0")),
                            ("dSCR1", sch.now("dSCR1")),
                            ("dSCR2", sch.now("dSCR2")),
                            ("dSCR3", sch.now("dSCR3"))]
    for j in range(NCH):
        g = emit_ffn_fc(j)
        if j == 0:
            emit_wcol()
        emit_ffn_mid(j)
        if j + 1 < NCH:
            cd = emit_attn_mm(j + 1)
        fw = emit_ffn_proj(j, g)
        emit_ffn_out(j, fw)
        if j + 1 < NCH:
            emit_attn_tail(j + 1, cd)

    sch.emit("sync", None, waits=[(s, sch.now(s)) for s in DSEMS
                                  if sch.now(s) > 0])

    with ctx:
        with nc.Block() as block:
            def runner(ename):
                def _(eng):
                    for (waits, fn, inc) in sch.ev[ename]:
                        for (s, v) in waits:
                            eng.wait_ge(sem[s], v)
                        if fn is None:
                            continue
                        ins = fn(eng)
                        if inc is not None:
                            ins.then_inc(sem[inc],
                                         16 if inc.startswith("d") else 1)
                return _
            block.sync(runner("sync"))
            block.scalar(runner("scalar"))
            block.gpsimd(runner("gpsimd"))
            block.tensor(runner("tensor"))
            block.vector(runner("vector"))
    return nc


_NC_CACHE = None


def kernel(L0_attn_X, L0_attn_S_n, L0_attn_W_Q, L0_attn_hh_b, L0_attn_gn_weight,
           L0_attn_gn_bias, L0_ffn_w_fc, L0_ffn_w_proj):
    global _NC_CACHE
    assert np.asarray(L0_attn_S_n).shape == (2, 2048)
    X = np.ascontiguousarray(np.asarray(L0_attn_X, dtype=np.float32))
    WQ = np.ascontiguousarray(np.asarray(L0_attn_W_Q, dtype=np.float32))
    hh = np.asarray(L0_attn_hh_b, dtype=np.float32)
    gnw = np.asarray(L0_attn_gn_weight, dtype=np.float32)
    wfc = np.asarray(L0_ffn_w_fc, dtype=np.float32)
    wpj = np.asarray(L0_ffn_w_proj, dtype=np.float32)

    wfcT = np.ascontiguousarray(wfc.T)
    wpjT = np.ascontiguousarray(wpj.T)

    t64 = np.arange(T, dtype=np.float64)
    inv = 10000.0 ** (-np.arange(0, 64, 2, dtype=np.float64) / 64.0)
    fr = np.repeat(t64[:, None] * inv[None, :], 2, axis=1)
    cosT = np.ascontiguousarray(np.cos(fr).astype(np.float32))
    sinT = np.ascontiguousarray(np.sin(fr).astype(np.float32))
    cosI = np.ones_like(cosT)
    sinI = np.zeros_like(sinT)
    identf = np.eye(128, dtype=np.float32)
    identh = np.eye(128, dtype=np.float32).astype(ml_dtypes.bfloat16)
    onesh = np.ones((128, 1), dtype=np.float32).astype(ml_dtypes.bfloat16)
    onesrow = np.ones((1, 128), dtype=np.float32)
    constv_np = np.tile(np.array(CONSTVALS, dtype=np.float32), (128, 1))

    if _NC_CACHE is None:
        _NC_CACHE = build_nc()
    nc = _NC_CACHE

    in_maps = []
    for core in range(8):
        b, m = core // 4, core % 4
        cols = slice(512 * m, 512 * (m + 1))
        in_maps.append({
            "xfull": X[b],
            "xres": np.ascontiguousarray(X[b, 4096 * m:4096 * (m + 1), :]),
            "wq": np.ascontiguousarray(WQ[:, cols]),
            "hhin": np.ascontiguousarray(hh[cols]).reshape(1, 512),
            "gnwin": np.ascontiguousarray(gnw[cols]).reshape(1, 512),
            "ropec": cosT if m == 0 else cosI,
            "ropes": sinT if m == 0 else sinI,
            "wfcT": wfcT,
            "wpjT": wpjT,
            "identf": identf,
            "identh": identh,
            "onesh": onesh,
            "onesrow": onesrow,
            "constv": constv_np,
        })
    res = run_bass_kernel_spmd(nc, in_maps, core_ids=list(range(8)))
    out = np.empty((2, T, C), dtype=np.float32)
    for core in range(8):
        b, m = core // 4, core % 4
        out[b, 4096 * m:4096 * (m + 1), :] = res.results[core]["out"]
    return out


# version 19 (final)
# speedup vs baseline: 1.0668x; 1.0668x over previous
import numpy as np
import ml_dtypes
from contextlib import ExitStack

import concourse.bass as bass
import concourse.mybir as mybir
from concourse.bass_utils import run_bass_kernel_spmd
from concourse.alu_op_type import AluOpType as ALU

F32 = mybir.dt.float32
BF16 = mybir.dt.bfloat16
FP8 = mybir.dt.float8e4
AF = mybir.ActivationFunctionType
AX = mybir.AxisListType.X

T = 16384
C = 2048
CONSTVALS = [0.0, 1.0, -0.5, -1.0, 1e-5, 1e-6, 0.5, 2.0]
NCH = 8
RT = 16
NELW = 2048 * 8192

ENGS = ("sync", "scalar", "gpsimd", "tensor", "vector")
ESEMS = {"tensor": "sPE", "vector": "sVE", "scalar": "sSE", "gpsimd": "sPO"}
DSEMS = (["dPRO", "dGNW", "dGAM", "dSTG0", "dSTG1", "dSCR0", "dSCR1",
          "dSCR2", "dSCR3",
          "dROPE", "dXF0", "dXF1", "dXT0", "dXT1", "dXRES", "dWF0", "dWF1",
          "dWP0", "dWP1", "dOUT", "dF0", "dF1", "dFR", "dOX1", "dOAC",
          "dSTAT", "dWCL", "dWCL1"]
         + ["dX2T_%d" % i for i in range(8)]
         + ["dXTs0", "dXTs1", "dXTp0", "dXTp1"]
         + ["dX2N_%d" % i for i in range(16)]
         + ["dXB%d_%d" % (j, p) for j in range(8) for p in range(2)])
ALLSEMS = tuple(ESEMS.values()) + tuple(DSEMS)


class Sch:
    def __init__(self):
        self.c = {s: 0 for s in ALLSEMS}
        self.ev = {e: [] for e in ENGS}
        self.lastw = {e: {s: 0 for s in ALLSEMS} for e in ENGS}

    def emit(self, eng, fn, inc=None, waits=()):
        if fn is not None and eng in ("vector", "scalar", "gpsimd"):
            waits = list(waits) + [(ESEMS[eng], self.c[ESEMS[eng]])]
        w = []
        for (s, v) in waits:
            if v > self.lastw[eng][s]:
                w.append((s, v))
                self.lastw[eng][s] = v
        if inc is not None:
            self.c[inc] += 16 if inc.startswith("d") else 1
        self.ev[eng].append((w, fn, inc))

    def now(self, s):
        return self.c[s]

    def barrier(self, snap=None):
        if snap is None:
            snap = dict(self.c)
        esems = tuple(ESEMS.values())
        for e in ENGS:
            own = ESEMS.get(e)
            ws = [(s, snap[s]) for s in esems if s != own and snap.get(s, 0) > 0]
            self.emit(e, None, waits=ws)
        return snap

    def snapshot(self):
        return dict(self.c)


def build_nc():
    nc = bass.Bass("TRN2", target_bir_lowering=False)

    xfull = nc.dram_tensor("xfull", [T, C], F32, kind="ExternalInput").ap()
    xres = nc.dram_tensor("xres", [4096, C], F32, kind="ExternalInput").ap()
    wq = nc.dram_tensor("wq", [C, 512], F32, kind="ExternalInput").ap()
    hhin = nc.dram_tensor("hhin", [1, 512], F32, kind="ExternalInput").ap()
    gnwin = nc.dram_tensor("gnwin", [1, 512], F32, kind="ExternalInput").ap()
    ropec = nc.dram_tensor("ropec", [T, 64], F32, kind="ExternalInput").ap()
    ropes = nc.dram_tensor("ropes", [T, 64], F32, kind="ExternalInput").ap()
    wfcT = nc.dram_tensor("wfcT", [2048, 8192], F32, kind="ExternalInput").ap()
    wpjT = nc.dram_tensor("wpjT", [8192, 2048], F32, kind="ExternalInput").ap()
    identf_d = nc.dram_tensor("identf", [128, 128], F32, kind="ExternalInput").ap()
    identh_d = nc.dram_tensor("identh", [128, 128], BF16, kind="ExternalInput").ap()
    onesh_d = nc.dram_tensor("onesh", [128, 1], BF16, kind="ExternalInput").ap()
    onesrow_d = nc.dram_tensor("onesrow", [1, 128], F32, kind="ExternalInput").ap()
    constv_d = nc.dram_tensor("constv", [128, 8], F32, kind="ExternalInput").ap()
    outd = nc.dram_tensor("out", [4096, C], F32, kind="ExternalOutput").ap()

    xbf_blk = nc.dram_tensor("xbf_blk", [16, T, 128], BF16, kind="Internal").ap()
    xbf_pl = nc.dram_tensor("xbf_pl", [T, 2048], BF16, kind="Internal").ap()
    x2n_blk = nc.dram_tensor("x2n_blk", [64, 512, 128], BF16, kind="Internal").ap()
    scr_fc = nc.dram_tensor("scr_fc", [16, 16, 128, 512], FP8, kind="Internal").ap()
    scr_pj = nc.dram_tensor("scr_pj", [4, 64, 128, 512], FP8, kind="Internal").ap()
    gscr = nc.dram_tensor("gscr", [2, 128], F32, kind="Internal").ap()
    sscr = nc.dram_tensor("sscr", [2, 4], F32, kind="Internal").ap()
    f_dram = nc.dram_tensor("f_dram", [4, 128, 2048], F32, kind="Internal").ap()
    stat_scr = nc.dram_tensor("stat_scr", [2, 512], F32, kind="Internal").ap()
    wcol_dram = nc.dram_tensor("wcol_dram", [4, 512], F32, kind="Internal").ap()

    ctx = ExitStack()
    SB = lambda n, sh, dt: ctx.enter_context(nc.sbuf_tensor(n, sh, dt))
    PS = lambda n, sh, dt: ctx.enter_context(nc.psum_tensor(n, sh, dt))

    garena = SB("garena", [128, 32768], BF16)
    GF = SB("GF", [128, 8192], F32)
    GTx1 = SB("GTx1", [128, 8192], F32)
    XhT = SB("XhT", [128, 8192], BF16)
    wqb = SB("wqb", [128, 8192], BF16)
    xf = [SB("xf0", [128, 2048], F32), SB("xf1", [128, 2048], F32)]
    xresb = SB("xresb", [128, 2048], F32)
    outb = SB("outb", [128, 2048], F32)
    u_bc = SB("u_bc", [128, 512], F32)
    hh_bc = SB("hh_bc", [128, 512], F32)
    tmpu = SB("tmpu", [128, 512], F32)
    tmpr = SB("tmpr", [128, 64], F32)
    rotb = SB("rotb", [128, 64], F32)
    gdump = SB("gdump", [128, 512], BF16)
    ropet_c = [SB("ropetc0", [128, 64], F32), SB("ropetc1", [128, 64], F32)]
    ropet_s = [SB("ropets0", [128, 64], F32), SB("ropets1", [128, 64], F32)]
    stats = SB("stats", [128, 940], F32)
    smalls = SB("smalls", [128, 32], F32)
    gnw_col = SB("gnw_col", [128, 8], F32)
    identf = SB("identfs", [128, 128], F32)
    identh = SB("idenths", [128, 128], BF16)
    onesh = SB("oneshs", [128, 1], BF16)
    onesrow = SB("onesrows", [1, 128], F32)
    constv = SB("constvs", [128, 8], F32)
    for _ci, _cval in enumerate(CONSTVALS):
        nc.const_aps.register(F32, _cval, constv[:, _ci:_ci + 1])
    rowt = SB("rowt", [2, 16], F32)
    srow = SB("srow", [1, 128], F32)
    hh_row = stats[0:1, 428:940]
    u_row = stats[0:1, 428:940]

    PP = [PS(f"pp{i}", [128, 512], F32) for i in range(8)]
    PB16 = [PP[3].bitcast(BF16), PP[5].bitcast(BF16)]

    sem = {s: ctx.enter_context(nc.semaphore(s)) for s in ALLSEMS}

    XT = [garena[:, 0:8192], garena[:, 8192:16384]]
    xb = [garena[:, 16384:18432], garena[:, 18432:20480]]
    xhat_t = garena[:, 20480:22528]
    ropec_sb = garena[:, 22528:24576].bitcast(F32)
    ropes_sb = garena[:, 24576:26624].bitcast(F32)
    wt_m1 = garena[:, 26624:28672]
    wt_out = [garena[:, 28672:30720], garena[:, 30720:32768]]
    fcw = [GF[:, 0:2048].bitcast(FP8), GF[:, 2048:4096].bitcast(FP8)]
    pjw = [XhT[:, 0:2048].bitcast(FP8), XhT[:, 2048:4096].bitcast(FP8)]
    stg_in = [xf[0][:, 0:1024], xf[1][:, 0:1024]]
    stg_out = [xf[0][:, 1024:1280].bitcast(FP8),
               xf[1][:, 1024:1280].bitcast(FP8)]
    stg_m1 = [xf[0][:, 1280:1536].bitcast(FP8),
              xf[1][:, 1280:1536].bitcast(FP8)]
    gam_in = [xresb[:, 0:1024], xresb[:, 1024:2048]]
    xfrow = [xf[0][:, 0:1024].bitcast(BF16), xf[0][:, 1024:2048].bitcast(BF16),
             xf[1][:, 0:1024].bitcast(BF16), xf[1][:, 1024:2048].bitcast(BF16)]
    dumpb = garena[:, 26624:28672]
    dumpf = XhT[:, 4096:6144]

    XS_SUM, XS_SQ = 0, 128
    S_R, S_CS = 256, 272
    S_VS, S_VQ = 288, 352
    S_V4, S_MUV, S_SR = 356, 360, 364
    S_X1S, S_X1Q, S_NM3, S_RHO = 368, 372, 376, 380
    S_GSUM, S_GSQP = 384, 448
    S_GNS, S_GNQ = 512, 640
    S_TMP = 768
    S_G4, S_GQ4, S_NM2, S_RH2 = 896, 900, 904, 908
    S_FS, S_FQ, S_F4, S_NM4, S_R4 = 912, 928, 932, 936, 940
    S_TM2 = 416

    M_CFC, M_GFC, M_NCFC = 0, 1, 2
    M_CPJ, M_GPJ, M_NCPJ = 3, 4, 5
    M_AFC, M_APJ, M_RED, M_GNW2 = 8, 9, 10, 16

    sch = Sch()

    def dma(eng, sm, out, in_, waits=(), slow=False, accum=False):
        kw = {"allow_slow_non_contiguous": True} if slow else {}
        if accum:
            kw["accum_op"] = ALU.add
        sch.emit(eng, (lambda e, o=out, i=in_, kw=kw: e.dma_start(out=o, in_=i,
                                                                  **kw)),
                 inc=sm, waits=waits)
        return sch.now(sm)

    def dmat(eng, sm, out, in_, waits=()):
        sch.emit(eng, (lambda e, o=out, i=in_: e.dma_start(out=o, in_=i,
                                                           transpose=True)),
                 inc=sm, waits=waits)
        return sch.now(sm)

    def op(eng, fn, waits=()):
        sch.emit(eng, fn, inc=ESEMS[eng], waits=waits)
        return sch.now(ESEMS[eng])

    c_id = dma("sync", "dPRO", identf[:], identf_d[:])
    dma("sync", "dPRO", identh[:], identh_d[:])
    c_on = dma("sync", "dPRO", onesh[:], onesh_d[:])
    c_or = dma("sync", "dPRO", onesrow[:], onesrow_d[:])
    dma("sync", "dPRO", constv[:], constv_d[:])
    c_hh = dma("sync", "dPRO", hh_row[:], hhin[:])
    c_gnw = dma("gpsimd", "dGNW", gnw_col[:, 0:4],
                gnwin[0, :].rearrange("(ct p) -> p ct", p=128), slow=True)
    for _kq in range(4):
        c_wq = dma("gpsimd", "dGNW",
                   wqb[:, _kq * 2048:(_kq + 1) * 2048].rearrange(
                       "p (k c) -> p k c", c=512),
                   wq.rearrange("(k p) c -> p k c", p=128)[:, 4 * _kq:4 * _kq + 4, :])

    c_pro = sch.now("dPRO")
    c_gnw = sch.now("dGNW")

    for k in range(16):
        op("tensor", (lambda t, k=k: t.matmul(PP[4][0:1, :], onesh[:],
                                              wqb[:, k * 512:(k + 1) * 512],
                                              start=(k == 0), stop=(k == 15))),
           waits=[("dGNW", c_gnw), ("dPRO", c_pro)])
    cb_hh = op("tensor", lambda t: t.matmul(PP[6][:, :], onesrow[:], hh_row[:],
                                            start=True, stop=True),
               waits=[("dPRO", c_pro)])
    op("vector", lambda v: v.tensor_copy(hh_bc[:], PP[6][:, :]),
       waits=[("sPE", cb_hh)])
    cvu = op("vector", lambda v: v.tensor_copy(u_row[:], PP[4][0:1, :]),
             waits=[("sPE", sch.now("sPE"))])
    cb_u = op("tensor", lambda t: t.matmul(PP[7][:, :], onesrow[:], u_row[:],
                                           start=True, stop=True),
              waits=[("sVE", cvu)])
    op("vector", lambda v: v.tensor_copy(u_bc[:], PP[7][:, :]),
       waits=[("sPE", cb_u)])
    op("vector", lambda v: v.tensor_tensor(
        out=smalls[:, M_GNW2:M_GNW2 + 4], in0=gnw_col[:, 0:4],
        in1=gnw_col[:, 0:4], op=ALU.mult), waits=[("dGNW", c_gnw)])

    chunk_cast = {}
    chunk_xbf = {}
    prev_cast = {}
    xbw = {}
    gam_free = {0: [], 1: []}
    op("vector", lambda v: v.memset(smalls[:, M_AFC:M_AFC + 2], 0.0))
    for g in range(NCH * RT):
        j, rt = divmod(g, 16)
        rows = 128 * g
        wv = []
        if g >= 2:
            wv += [("sVE", prev_cast[g - 2][0]), ("sSE", prev_cast[g - 2][1])]
        cL = dma("sync", "dXF%d" % (g % 2), xf[g % 2][:],
                 xfull[rows:rows + 128, :], waits=wv)
        wv2 = [("dXF%d" % (g % 2), cL)]
        if g >= 2:
            j2, rt2 = divmod(g - 2, 16)
            wv2.append(("dXB%d_%d" % (j2, rt2 % 2), xbw[g - 2]))
        cV = op("vector", (lambda v, g_=g: v.tensor_scalar(
            out=xb[g_ % 2], in0=xf[g_ % 2][:], scalar1=1.0, scalar2=None,
            op0=ALU.mult, op1=ALU.add,
            accum_out=stats[:, XS_SUM + g_:XS_SUM + g_ + 1])), waits=wv2)
        cS2 = op("scalar", (lambda s, g_=g: s.activation(
            out=dumpb, in_=xf[g_ % 2][:], func=AF.Square,
            accum_out=stats[:, XS_SQ + g_:XS_SQ + g_ + 1])),
            waits=[("dXF%d" % (g % 2), cL), ("dPRO", c_pro)])
        prev_cast[g] = (cV, cS2)
        xbw[g] = dma("gpsimd", "dXB%d_%d" % (j, rt % 2),
                     xbf_pl[rows:rows + 128, :], xb[g % 2],
                     waits=[("sVE", cV)])
        if j == 0:
            for kq in range(4):
                xbw[g] = dma(
                    "gpsimd", "dXB%d_%d" % (j, rt % 2),
                    xbf_blk[4 * kq:4 * kq + 4, rows:rows + 128, :].rearrange(
                        "kb r c -> r kb c"),
                    xb[g % 2][:, kq * 512:(kq + 1) * 512].rearrange(
                        "p (kb c) -> p kb c", c=128))
        if rt == RT - 1:
            chunk_cast[j] = (cV, cS2)
            chunk_xbf[j] = (sch.now("dXB%d_0" % j), sch.now("dXB%d_1" % j))
        p = g % 2
        t, h = divmod(g, 8)
        cgl = dma("scalar", "dSTG%d" % p, gam_in[p],
                  wfcT[t * 128:(t + 1) * 128, h * 1024:(h + 1) * 1024],
                  waits=gam_free[p])
        cgr = op("vector", (lambda v, p_=p: v.tensor_reduce(
            smalls[:, M_RED:M_RED + 1], gam_in[p_], axis=AX, op=ALU.add,
            apply_absolute_value=True)), waits=[("dSTG%d" % p, cgl)])
        op("vector", lambda v: v.tensor_tensor(
            smalls[:, M_AFC:M_AFC + 1], smalls[:, M_AFC:M_AFC + 1],
            smalls[:, M_RED:M_RED + 1], op=ALU.add))
        gam_free[p] = [("sVE", cgr)]
    xv_end = prev_cast[NCH * RT - 1][0]
    xs_end = prev_cast[NCH * RT - 1][1]

    op("vector", lambda v: v.tensor_scalar(
        out=stats[:, XS_SUM:XS_SUM + 128], in0=stats[:, XS_SUM:XS_SUM + 128],
        scalar1=1.0 / 2048, scalar2=None, op0=ALU.mult),
       waits=[("sVE", xv_end), ("sSE", xs_end)])
    op("vector", lambda v: v.tensor_scalar(
        out=stats[:, XS_SQ:XS_SQ + 128], in0=stats[:, XS_SQ:XS_SQ + 128],
        scalar1=1.0 / 2048, scalar2=None, op0=ALU.mult))
    op("vector", lambda v: v.tensor_tensor(
        out=stats[:, S_TMP:S_TMP + 128], in0=stats[:, XS_SUM:XS_SUM + 128],
        in1=stats[:, XS_SUM:XS_SUM + 128], op=ALU.mult))
    cRv = op("vector", lambda v: v.tensor_tensor(
        out=stats[:, XS_SQ:XS_SQ + 128], in0=stats[:, XS_SQ:XS_SQ + 128],
        in1=stats[:, S_TMP:S_TMP + 128], op=ALU.subtract))
    cRl = op("scalar", lambda s: s.activation(
        out=stats[:, S_TMP:S_TMP + 128], in_=stats[:, XS_SQ:XS_SQ + 128],
        func=AF.Ln, bias=1e-5, scale=1.0), waits=[("sVE", cRv), ("dPRO", c_pro)])
    cRe = op("scalar", lambda s: s.activation(
        out=stats[:, XS_SQ:XS_SQ + 128], in_=stats[:, S_TMP:S_TMP + 128],
        func=AF.Exp, bias=0.0, scale=-0.5))
    rstats_ve = op("vector", lambda v: v.tensor_tensor(
        out=stats[:, XS_SUM:XS_SUM + 128], in0=stats[:, XS_SUM:XS_SUM + 128],
        in1=stats[:, XS_SQ:XS_SQ + 128], op=ALU.mult), waits=[("sSE", cRe)])
    rstats_se = cRe

    stg_free = {0: [("sVE", xv_end), ("sSE", xs_end)],
                1: [("sVE", xv_end), ("sSE", xs_end)]}
    scrw = {0: 0, 1: 0, 2: 0, 3: 0}
    m1_free = {0: 0, 1: 0}
    gam_done = {}

    def emit_gamma_pj():
        for u in range(128):
            p = u % 2
            t, h = divmod(u, 2)
            ap = wpjT[t * 128:(t + 1) * 128, h * 1024:(h + 1) * 1024]
            cL = dma("scalar", "dSTG%d" % p, gam_in[p], ap, waits=gam_free[p])
            cR = op("vector", (lambda v, p_=p: v.tensor_reduce(
                smalls[:, M_RED:M_RED + 1], gam_in[p_], axis=AX, op=ALU.add,
                apply_absolute_value=True)), waits=[("dSTG%d" % p, cL)])
            op("vector", lambda v: v.tensor_tensor(
                smalls[:, M_APJ:M_APJ + 1], smalls[:, M_APJ:M_APJ + 1],
                smalls[:, M_RED:M_RED + 1], op=ALU.add))
            gam_free[p] = [("sVE", cR)]

    def emit_gfin(mat):
        idx = 0 if mat == "fc" else 1
        acc = M_AFC + idx
        base = M_CFC if mat == "fc" else M_CPJ
        c1 = dma("sync", "dGAM", gscr[idx, :], smalls[:, acc:acc + 1],
                 waits=[("sVE", sch.now("sVE"))])
        c2 = dma("sync", "dGAM", srow[0:1, 0:128], gscr[idx:idx + 1, :],
                 waits=[("dGAM", c1)])
        op("vector", lambda v: v.reduce_sum(rowt[0:1, 0:1], srow[0:1, 0:128],
                                            axis=AX),
           waits=[("dGAM", c2)])
        op("vector", lambda v: v.tensor_scalar(
            out=rowt[0:1, 1:2], in0=rowt[0:1, 0:1], scalar1=0.5 / NELW,
            scalar2=0.5e-6, op0=ALU.mult, op1=ALU.add))
        gsc = 0.5 / NELW if mat == "fc" else -1.0 / NELW
        op("vector", (lambda v, g_=gsc: v.tensor_scalar(
            out=rowt[0:1, 2:3], in0=rowt[0:1, 0:1], scalar1=g_,
            scalar2=None, op0=ALU.mult)))
        cN = op("vector", lambda v: v.tensor_scalar(
            out=rowt[0:1, 3:4], in0=rowt[0:1, 1:2], scalar1=-1.0,
            scalar2=None, op0=ALU.mult))
        c3 = dma("sync", "dGAM", sscr[idx, 0:3], rowt[0:1, 1:4],
                 waits=[("sVE", cN)])
        c4 = dma("sync", "dGAM", smalls[:, base:base + 3],
                 sscr[idx:idx + 1, 0:3].broadcast_to((128, 3)),
                 waits=[("dGAM", c3)])
        gam_done[mat] = c4

    def emit_tern(mat):
        cgam = gam_done[mat]
        for un in range(128):
            p = un % 2
            scr_sem = "dSCR%d" % (p if mat == "fc" else 2 + p)
            si = p if mat == "fc" else 2 + p
            if mat == "fc":
                ob, kq = divmod(un, 8)
                src3 = wfcT.rearrange("(k p) o -> p k o", p=128)[
                    :, 2 * kq:2 * kq + 2, ob * 512:(ob + 1) * 512]
                dst3 = scr_fc[ob, 2 * kq:2 * kq + 2].rearrange("k p c -> p k c")
            else:
                cbq, u32 = divmod(un, 32)
                src3 = wpjT.rearrange("(o p) c -> p o c", p=128)[
                    :, 2 * u32:2 * u32 + 2, cbq * 512:(cbq + 1) * 512]
                dst3 = scr_pj[cbq, 2 * u32:2 * u32 + 2].rearrange(
                    "o p c -> p o c")
            cL = dma("sync", "dSTG%d" % (2 + p) if False else "dSTG%d" % p,
                     stg_in[p].rearrange("p (k c) -> p k c", c=512), src3,
                     waits=stg_free[p])
            if mat == "fc":
                cSa = op("scalar", (lambda s, p_=p: s.activation(
                    out=stg_m1[p_], in_=stg_in[p_], func=AF.Sign,
                    bias=smalls[:, M_NCFC:M_NCFC + 1], scale=1.0)),
                    waits=[("dSTG%d" % p, cL), ("dGAM", cgam),
                           ("sPO", m1_free[p])])
                cSb = op("scalar", (lambda s, p_=p: s.activation(
                    out=stg_out[p_], in_=stg_in[p_], func=AF.Sign,
                    bias=smalls[:, M_CFC:M_CFC + 1], scale=1.0)),
                    waits=[(scr_sem, scrw[si])])
                cP = op("gpsimd", (lambda g, p_=p: g.tensor_tensor(
                    out=stg_out[p_], in0=stg_out[p_], in1=stg_m1[p_],
                    op=ALU.add)), waits=[("sSE", cSb)])
                stg_free[p] = [("sSE", cSb)]
                m1_free[p] = cP
                wdep = [("sPO", cP)]
            else:
                wm = [("dSTG%d" % p, cL), ("dGAM", cgam),
                      ("sPO", m1_free[p])]
                if un < 2:
                    wm.append(("dSCR%d" % p, sch.now("dSCR%d" % p)))
                cM = op("vector", (lambda v, p_=p: v.tensor_scalar(
                    out=stg_m1[p_], in0=stg_in[p_],
                    scalar1=smalls[:, M_CPJ:M_CPJ + 1], scalar2=None,
                    op0=ALU.is_gt)), waits=wm)
                cS = op("vector", (lambda v, p_=p: v.scalar_tensor_tensor(
                    out=stg_out[p_], in0=stg_in[p_],
                    scalar=smalls[:, M_NCPJ:M_NCPJ + 1], in1=stg_m1[p_],
                    op0=ALU.is_lt, op1=ALU.subtract)),
                    waits=[(scr_sem, scrw[si])])
                stg_free[p] = [("sVE", cS)]
                wdep = [("sVE", cS)]
            cW = dma("scalar", scr_sem, dst3,
                     stg_out[p].rearrange("p (k c) -> p k c", c=512),
                     waits=wdep)
            scrw[si] = cW

    state = {"dOX1": 0, "dOAC": 0, "dump_last": 0, "dump_se": 0, "dF_last": [],
             "pjevac_last": 0, "proj_mm_last": 0, "gn_apply_done": 0,
             "fc_pe_last": 0, "gelu_last": 0, "xres_free": 0, "outb_free_se": 0,
             "outb_free_ve": 0, "x2t_read_done": 0, "stg_handoff": [],
             "xtev": (0, 0), "x1w": {}, "attn_pe_last": 0,
             "xh_done_ve": 0, "xh_done_se": 0, "fcstat_evac": 0,
             "rho_col": 0, "muro_ve": 0, "wcol_pe": 0, "wcol_done": 0}
    rope_free = {0: 0, 1: 0}
    xbws = [("dXB%d_%d" % (jj, p), chunk_xbf[jj][p])
            for jj in range(NCH) for p in range(2)]

    fcw8 = [outb[:].bitcast(FP8), GTx1[:, 4096:6144].bitcast(FP8)]
    pjw8 = [xresb[:, 0:1024].bitcast(FP8), xresb[:, 1024:2048].bitcast(FP8)]
    sqring = [GTx1[:, i * 256:(i + 1) * 256].bitcast(BF16) for i in range(8)]
    ROWR = ([GTx1[0:1, 2048 + i * 512:2560 + i * 512] for i in range(4)]
            + [GTx1[0:1, 6144 + i * 512:6656 + i * 512] for i in range(4)])
    xt_row = [xf[0][:, 0:1024].bitcast(BF16), xf[0][:, 1024:2048].bitcast(BF16)]
    xt_ring = xf[1][:, 0:1024].bitcast(BF16)
    xh_ring = xf[1][:, 1024:2048].bitcast(BF16)
    PBX = [PP[0].bitcast(BF16), PP[2].bitcast(BF16)]
    fstg = [GTx1[:, 0:512], GTx1[:, 512:1024]]

    def emit_attn_mm(j):
        bank_free = {}
        chain_done = 0
        xrow_free = {0: [], 1: []}
        cXT = {}
        mm_hist = {}
        rope_cnt = {}
        pbx_ev = {0: ("sVE", 0), 1: ("sVE", 0)}

        def emit_chain(rt):
            gsl = GF[:, rt * 512:(rt + 1) * 512]
            wv0 = [("sPE", mm_hist[rt])]
            if rt == 0:
                wv0 += [("sVE", rstats_ve), ("sSE", rstats_se)]
            op("vector", (lambda v, j_=j, rt_=rt: v.tensor_scalar(
                out=tmpu[:], in0=u_bc[:],
                scalar1=stats[:, XS_SUM + 16 * j_ + rt_:
                              XS_SUM + 16 * j_ + rt_ + 1],
                scalar2=None, op0=ALU.mult)), waits=wv0)
            bank_free[rt] = ("sVE", op("vector", (lambda v, j_=j, rt_=rt,
                                                  g_=gsl:
                                                  v.scalar_tensor_tensor(
                out=g_, in0=PP[6 + rt_ % 2][:, :],
                scalar=stats[:, XS_SQ + 16 * j_ + rt_:
                             XS_SQ + 16 * j_ + rt_ + 1], in1=tmpu[:],
                op0=ALU.mult, op1=ALU.subtract))))
            op("vector", (lambda v, g_=gsl: v.tensor_scalar(
                out=rotb[:, 0:64:2], in0=g_[:, 1:64:2], scalar1=-1.0,
                scalar2=None, op0=ALU.mult)),
               waits=[("dROPE", sch.now("dROPE"))])
            op("vector", (lambda v, g_=gsl: v.tensor_scalar(
                out=rotb[:, 1:64:2], in0=g_[:, 0:64:2], scalar1=1.0,
                scalar2=None, op0=ALU.mult)))
            op("vector", (lambda v, g_=gsl, rt_=rt: v.tensor_tensor(
                out=tmpr[:], in0=g_[:, 0:64], in1=ropet_c[rt_ % 2][:],
                op=ALU.mult)))
            op("vector", (lambda v, rt_=rt: v.tensor_tensor(
                out=rotb[:], in0=rotb[:], in1=ropet_s[rt_ % 2][:],
                op=ALU.mult)))
            rope_free[rt % 2] = op("vector", (lambda v, g_=gsl: v.tensor_tensor(
                out=g_[:, 0:64], in0=tmpr[:], in1=rotb[:], op=ALU.add)))
            op("vector", (lambda v, g_=gsl: v.tensor_scalar(
                out=g_, in0=g_, scalar1=0.0, scalar2=None, op0=ALU.max)))
            cHH = op("vector", (lambda v, g_=gsl: v.tensor_tensor(
                out=g_, in0=g_, in1=hh_bc[:], op=ALU.add)))
            cTH = op("scalar", (lambda s, g_=gsl: s.activation(
                out=g_, in_=g_, func=AF.Tanh)), waits=[("sVE", cHH)])
            cSQ = op("scalar", (lambda s, g_=gsl: s.activation(
                out=gdump[:], in_=g_, func=AF.Square)),
                waits=[("sVE", state["dump_last"])])
            op("vector", (lambda v, g_=gsl, rt_=rt: v.tensor_reduce(
                out=stats[:, S_GNS + rt_ * 8:S_GNS + rt_ * 8 + 8],
                in_=g_.rearrange("p (g k) -> p g k", k=64), axis=AX,
                op=ALU.add)), waits=[("sSE", cTH)])
            return op("vector", (lambda v, rt_=rt: v.tensor_reduce(
                out=stats[:, S_GNQ + rt_ * 8:S_GNQ + rt_ * 8 + 8],
                in_=gdump[:].rearrange("p (g k) -> p g k", k=64),
                axis=AX, op=ALU.add)), waits=[("sSE", cSQ)])

        for rt in range(RT):
            rows = 2048 * j + 128 * rt
            dma("scalar", "dROPE", ropet_c[rt % 2][:], ropec[rows:rows + 128, :],
                waits=[("sVE", rope_free[rt % 2])])
            rope_cnt[rt] = dma("scalar", "dROPE", ropet_s[rt % 2][:],
                               ropes[rows:rows + 128, :])
            if j == 0:
                if rt % 4 == 0:
                    grp = rt // 4
                    wv = [("dXB%d_0" % j, chunk_xbf[j][0]),
                          ("dXB%d_1" % j, chunk_xbf[j][1])]
                    if grp >= 2:
                        wv.append(("sPE", mm_hist[4 * (grp - 2) + 3]))
                    c0 = 0
                    for kb in range(16):
                        c0 = dmat("scalar", "dXT%d" % (grp % 2),
                                  XT[grp % 2][:, kb * 512:(kb + 1) * 512],
                                  xbf_blk[kb, 512 * grp:512 * (grp + 1), :],
                                  waits=wv)
                        wv = []
                    cXT[grp] = c0
                evv = evs = 0
            else:
                slot = rt % 2
                wvr = ([("dXB%d_0" % j, chunk_xbf[j][0]),
                        ("dXB%d_1" % j, chunk_xbf[j][1])]
                       + (state["stg_handoff"] if rt < 2 else [])
                       + ([("sPE", state["attn_pe_last"])] if rt < 2 else [])
                       + xrow_free[slot])
                xsem = "dXTs%d" % slot
                cXR = dma("sync", xsem, xt_row[slot],
                          xbf_pl[rows:rows + 128, :], waits=wvr)
                evv = evs = 0
                for kb in range(16):
                    s2 = kb % 2
                    wvt = [(xsem, cXR), pbx_ev[s2]]
                    if rt == 0 and kb < 2:
                        wvt.append(("sSE", state["gelu_last"]))
                        wvt.append(("sVE", state["pjevac_last"]))
                    cTx = op("tensor", (lambda t, k_=kb, s_=s2, sl_=slot:
                                        t.transpose(
                        PBX[s_][:, 0:128],
                        xt_row[sl_][:, k_ * 128:(k_ + 1) * 128],
                        identh[:])), waits=wvt)
                    if kb % 2 == 0:
                        evv = op("vector", (lambda v, k_=kb, s_=s2:
                                            v.tensor_copy(
                            xt_ring[:, k_ * 128:(k_ + 1) * 128],
                            PBX[s_][:, 0:128])), waits=[("sPE", cTx)])
                        pbx_ev[s2] = ("sVE", evv)
                    else:
                        evs = op("scalar", (lambda s, k_=kb, s_=s2: s.copy(
                            xt_ring[:, k_ * 128:(k_ + 1) * 128],
                            PBX[s_][:, 0:128])), waits=[("sPE", cTx)])
                        pbx_ev[s2] = ("sSE", evs)
                xrow_free[slot] = [("sPE", sch.now("sPE"))]
            if j == 0:
                wvm = [("dXT%d" % ((rt // 4) % 2), cXT[rt // 4])]
                if rt < 2:
                    wvm.append(("sVE", rstats_ve))
            else:
                wvm = [("sVE", evv), ("sSE", evs)]
                if rt < 2:
                    wvm.append(("sVE", state["fcstat_evac"]))
            if rt >= 2:
                wvm.append(bank_free[rt - 2])
            mm_last = 0
            for kb in range(16):
                if j == 0:
                    lhs = XT[(rt // 4) % 2][:, kb * 512 + (rt % 4) * 128:
                                            kb * 512 + (rt % 4) * 128 + 128]
                else:
                    lhs = xt_ring[:, kb * 128:(kb + 1) * 128]
                mm_last = op("tensor", (lambda t, l_=lhs, k=kb, rt_=rt:
                                        t.matmul(
                    PP[6 + rt_ % 2][:, :], l_,
                    wqb[:, k * 512:(k + 1) * 512],
                    start=(k == 0), stop=(k == 15))), waits=wvm)
                wvm = []
            mm_hist[rt] = mm_last
            if rt >= 1:
                chain_done = emit_chain(rt - 1)
        chain_done = emit_chain(RT - 1)
        state["attn_pe_last"] = sch.now("sPE")
        state["xtev"] = (sch.now("sVE"), sch.now("sSE"))
        state["dump_se"] = sch.now("sSE")
        return chain_done

    def emit_attn_tail(j, chain_done):
        op("vector", lambda v: v.tensor_scalar(
            out=stats[:, S_TMP:S_TMP + 128], in0=stats[:, S_GNQ:S_GNQ + 128],
            scalar1=1.0 / 64, scalar2=None, op0=ALU.mult),
           waits=[("sVE", chain_done)])
        op("vector", lambda v: v.tensor_scalar(
            out=stats[:, S_GNS:S_GNS + 128], in0=stats[:, S_GNS:S_GNS + 128],
            scalar1=1.0 / 64, scalar2=None, op0=ALU.mult))
        op("vector", lambda v: v.tensor_tensor(
            out=stats[:, S_GNQ:S_GNQ + 128], in0=stats[:, S_GNS:S_GNS + 128],
            in1=stats[:, S_GNS:S_GNS + 128], op=ALU.mult))
        cVv = op("vector", lambda v: v.tensor_tensor(
            out=stats[:, S_GNQ:S_GNQ + 128], in0=stats[:, S_TMP:S_TMP + 128],
            in1=stats[:, S_GNQ:S_GNQ + 128], op=ALU.subtract))
        cLg = op("scalar", lambda s: s.activation(
            out=stats[:, S_TMP:S_TMP + 128], in_=stats[:, S_GNQ:S_GNQ + 128],
            func=AF.Ln, bias=1e-5, scale=1.0), waits=[("sVE", cVv)])
        cEg = op("scalar", lambda s: s.activation(
            out=stats[:, S_GNQ:S_GNQ + 128], in_=stats[:, S_TMP:S_TMP + 128],
            func=AF.Exp, bias=0.0, scale=-0.5))
        op("vector", lambda v: v.tensor_scalar(
            out=stats[:, S_GNS:S_GNS + 128], in0=stats[:, S_GNS:S_GNS + 128],
            scalar1=-1.0, scalar2=None, op0=ALU.mult), waits=[("sSE", cEg)])
        cNMR = op("vector", lambda v: v.tensor_tensor(
            out=stats[:, S_GNS:S_GNS + 128], in0=stats[:, S_GNS:S_GNS + 128],
            in1=stats[:, S_GNQ:S_GNQ + 128], op=ALU.mult))
        gn_apply_done = 0
        gt_slot = {0: ("sVE", 0), 1: ("sVE", 0), 2: ("sVE", 0), 3: ("sVE", 0)}
        tr_done = 0
        for ct in range(4):
            for rtb in range(RT):
                i = ct * RT + rtb
                slot4 = i % 4
                bank4 = [1, 3, 4, 5][slot4]
                gn_ve = 0
                for gh in range(2):
                    col = rtb * 8 + 2 * ct + gh
                    if gh == 0:
                        gn_ve = op("vector", (lambda v, rt_=rtb,
                                              g_=2 * ct + gh,
                                              c_=col: v.tensor_scalar(
                            out=GF[:, rt_ * 512 + g_ * 64:
                                   rt_ * 512 + g_ * 64 + 64],
                            in0=GF[:, rt_ * 512 + g_ * 64:
                                   rt_ * 512 + g_ * 64 + 64],
                            scalar1=stats[:, S_GNQ + c_:S_GNQ + c_ + 1],
                            scalar2=stats[:, S_GNS + c_:S_GNS + c_ + 1],
                            op0=ALU.mult, op1=ALU.add)),
                            waits=([("sVE", cNMR)] if i == 0 else []))
                    else:
                        gn_apply_done = op("scalar", (lambda s, rt_=rtb,
                                                      g_=2 * ct + gh,
                                                      c_=col: s.activation(
                            out=GF[:, rt_ * 512 + g_ * 64:
                                   rt_ * 512 + g_ * 64 + 64],
                            in_=GF[:, rt_ * 512 + g_ * 64:
                                   rt_ * 512 + g_ * 64 + 64],
                            func=AF.Identity,
                            bias=stats[:, S_GNS + c_:S_GNS + c_ + 1],
                            scale=stats[:, S_GNQ + c_:S_GNQ + c_ + 1])),
                            waits=([("sVE", cNMR)] if i == 0 else []))
                wv = [("sSE", gn_apply_done), ("sVE", gn_ve),
                      ("sVE", state["pjevac_last"]),
                      gt_slot[slot4]] + (state["dF_last"] if i < 2 else [])
                cTt = op("tensor", (lambda t, c_=ct, r_=rtb, b_=bank4:
                                    t.transpose(
                    PP[b_][:, 0:128],
                    GF[:, r_ * 512 + c_ * 128:r_ * 512 + c_ * 128 + 128],
                    identf[:])), waits=wv)
                if i % 2 == 0:
                    c = op("vector", (lambda v, c_=ct, r_=rtb, b_=bank4:
                                      v.tensor_scalar(
                        out=GTx1[:, c_ * 2048 + r_ * 128:
                                 c_ * 2048 + r_ * 128 + 128],
                        in0=PP[b_][:, 0:128],
                        scalar1=1.0, scalar2=None, op0=ALU.mult, op1=ALU.add,
                        accum_out=stats[:, S_VS + c_ * 16 + r_:
                                        S_VS + c_ * 16 + r_ + 1])),
                        waits=[("sPE", cTt)])
                    gt_slot[slot4] = ("sVE", c)
                else:
                    c = op("scalar", (lambda s, c_=ct, r_=rtb, b_=bank4:
                                      s.activation(
                        out=GTx1[:, c_ * 2048 + r_ * 128:
                                 c_ * 2048 + r_ * 128 + 128],
                        in_=PP[b_][:, 0:128], func=AF.Identity,
                        accum_out=stats[:, S_VS + c_ * 16 + r_:
                                        S_VS + c_ * 16 + r_ + 1])),
                        waits=[("sPE", cTt)])
                    gt_slot[slot4] = ("sSE", c)
                tr_done = c
        vsq_done = 0
        for ct in range(4):
            for qq in range(4):
                vsq_done = op("vector", (lambda v, c_=ct, q_=qq:
                                         v.scalar_tensor_tensor(
                    out=gdump[:],
                    in0=GTx1[:, c_ * 2048 + q_ * 512:c_ * 2048 + q_ * 512 + 512],
                    scalar=1.0,
                    in1=GTx1[:, c_ * 2048 + q_ * 512:c_ * 2048 + q_ * 512 + 512],
                    op0=ALU.mult, op1=ALU.mult,
                    accum_out=stats[:, S_GSUM + c_ * 4 + q_:
                                    S_GSUM + c_ * 4 + q_ + 1])),
                    waits=([("sSE", sch.now("sSE"))] if ct == 0 and qq == 0
                           else []))
        op("vector", lambda v: v.tensor_reduce(
            out=stats[:, S_VQ:S_VQ + 4],
            in_=stats[:, S_GSUM:S_GSUM + 16].rearrange("p (a b) -> p a b", b=4),
            axis=AX, op=ALU.add), waits=[("sVE", vsq_done)])
        op("vector", lambda v: v.tensor_reduce(
            out=stats[:, S_V4:S_V4 + 4],
            in_=stats[:, S_VS:S_VS + 64].rearrange("p (a b) -> p a b", b=16),
            axis=AX, op=ALU.add))
        op("vector", lambda v: v.tensor_scalar(
            out=stats[:, S_MUV:S_MUV + 4], in0=stats[:, S_V4:S_V4 + 4],
            scalar1=1.0 / 2048, scalar2=None, op0=ALU.mult))
        op("vector", lambda v: v.tensor_scalar(
            out=stats[:, S_TMP:S_TMP + 4], in0=stats[:, S_VQ:S_VQ + 4],
            scalar1=1.0 / 2048, scalar2=None, op0=ALU.mult))
        op("vector", lambda v: v.tensor_tensor(
            out=stats[:, S_TMP + 4:S_TMP + 8], in0=stats[:, S_MUV:S_MUV + 4],
            in1=stats[:, S_MUV:S_MUV + 4], op=ALU.mult))
        op("vector", lambda v: v.tensor_tensor(
            out=stats[:, S_TMP:S_TMP + 4], in0=stats[:, S_TMP:S_TMP + 4],
            in1=stats[:, S_TMP + 4:S_TMP + 8], op=ALU.subtract))
        cVg = op("vector", lambda v: v.tensor_tensor(
            out=stats[:, S_TMP:S_TMP + 4], in0=stats[:, S_TMP:S_TMP + 4],
            in1=smalls[:, M_GNW2:M_GNW2 + 4], op=ALU.mult))
        cLv = op("scalar", lambda s: s.activation(
            out=stats[:, S_TMP + 4:S_TMP + 8], in_=stats[:, S_TMP:S_TMP + 4],
            func=AF.Ln, bias=1e-5, scale=1.0), waits=[("sVE", cVg)])
        cEv2 = op("scalar", lambda s: s.activation(
            out=stats[:, S_TMP:S_TMP + 4], in_=stats[:, S_TMP + 4:S_TMP + 8],
            func=AF.Exp, bias=0.0, scale=-0.5))
        cSr = op("vector", lambda v: v.tensor_tensor(
            out=stats[:, S_SR:S_SR + 4], in0=gnw_col[:, 0:4],
            in1=stats[:, S_TMP:S_TMP + 4], op=ALU.mult), waits=[("sSE", cEv2)])
        cNmv = op("vector", lambda v: v.tensor_scalar(
            out=stats[:, S_MUV:S_MUV + 4], in0=stats[:, S_MUV:S_MUV + 4],
            scalar1=-1.0, scalar2=None, op0=ALU.mult))
        x1_done = {}
        for ct in range(4):
            r0 = j + 1024 * ct
            cXr = dma("scalar", "dXRES", xresb[:], xres[r0:r0 + 1017:8, :],
                      waits=([("sVE", x1_done.get(ct - 1, 0)),
                              ("sPE", state["proj_mm_last"])]
                             + (gam_free[0] + gam_free[1]
                                if ct == 0 and j == 0 else [])))
            cA1 = op("vector", (lambda v, c_=ct: v.tensor_scalar(
                out=GTx1[:, c_ * 2048:(c_ + 1) * 2048],
                in0=GTx1[:, c_ * 2048:(c_ + 1) * 2048],
                scalar1=stats[:, S_MUV + c_:S_MUV + c_ + 1],
                scalar2=stats[:, S_SR + c_:S_SR + c_ + 1],
                op0=ALU.add, op1=ALU.mult)),
                waits=[("sVE", cSr), ("sVE", cNmv)])
            cA2 = op("vector", (lambda v, c_=ct: v.scalar_tensor_tensor(
                out=GTx1[:, c_ * 2048:(c_ + 1) * 2048],
                in0=GTx1[:, c_ * 2048:(c_ + 1) * 2048], scalar=1.0,
                in1=xresb[:], op0=ALU.mult, op1=ALU.add,
                accum_out=stats[:, S_X1S + c_:S_X1S + c_ + 1])),
                waits=[("dXRES", cXr)])
            state["x1w"][(j, ct)] = dma(
                "scalar", "dOX1", outd[r0:r0 + 1017:8, :],
                GTx1[:, ct * 2048:(ct + 1) * 2048], waits=[("sVE", cA2)])
            x1_done[ct] = op("vector", (lambda v, c_=ct:
                                        v.scalar_tensor_tensor(
                out=gdump[:],
                in0=GTx1[:, c_ * 2048:c_ * 2048 + 512], scalar=1.0,
                in1=GTx1[:, c_ * 2048:c_ * 2048 + 512], op0=ALU.mult,
                op1=ALU.mult,
                accum_out=stats[:, S_GSUM + c_:S_GSUM + c_ + 1])))
            for qq in range(1, 4):
                x1_done[ct] = op("vector", (lambda v, c_=ct, q_=qq:
                                            v.scalar_tensor_tensor(
                    out=gdump[:],
                    in0=GTx1[:, c_ * 2048 + q_ * 512:
                             c_ * 2048 + q_ * 512 + 512], scalar=1.0,
                    in1=GTx1[:, c_ * 2048 + q_ * 512:
                             c_ * 2048 + q_ * 512 + 512], op0=ALU.mult,
                    op1=ALU.mult,
                    accum_out=stats[:, S_GSUM + 16 + c_ * 3 + q_ - 1:
                                    S_GSUM + 16 + c_ * 3 + q_])))
        state["xres_free"] = x1_done[3]
        state["dump_last"] = x1_done[3]
        op("vector", lambda v: v.tensor_reduce(
            out=stats[:, S_X1Q:S_X1Q + 4],
            in_=stats[:, S_GSUM + 16:S_GSUM + 16 + 12].rearrange(
                "p (a b) -> p a b", b=3),
            axis=AX, op=ALU.add), waits=[("sVE", x1_done[3])])
        op("vector", lambda v: v.tensor_tensor(
            out=stats[:, S_X1Q:S_X1Q + 4], in0=stats[:, S_X1Q:S_X1Q + 4],
            in1=stats[:, S_GSUM:S_GSUM + 4], op=ALU.add))
        op("vector", lambda v: v.tensor_scalar(
            out=stats[:, S_TMP:S_TMP + 4], in0=stats[:, S_X1S:S_X1S + 4],
            scalar1=1.0 / 2048, scalar2=None, op0=ALU.mult))
        op("vector", lambda v: v.tensor_scalar(
            out=stats[:, S_TMP + 4:S_TMP + 8], in0=stats[:, S_X1Q:S_X1Q + 4],
            scalar1=1.0 / 2048, scalar2=None, op0=ALU.mult))
        op("vector", lambda v: v.tensor_tensor(
            out=stats[:, S_TMP + 8:S_TMP + 12], in0=stats[:, S_TMP:S_TMP + 4],
            in1=stats[:, S_TMP:S_TMP + 4], op=ALU.mult))
        cV3 = op("vector", lambda v: v.tensor_tensor(
            out=stats[:, S_TMP + 4:S_TMP + 8], in0=stats[:, S_TMP + 4:S_TMP + 8],
            in1=stats[:, S_TMP + 8:S_TMP + 12], op=ALU.subtract))
        cL3 = op("scalar", lambda s: s.activation(
            out=stats[:, S_TMP + 8:S_TMP + 12],
            in_=stats[:, S_TMP + 4:S_TMP + 8],
            func=AF.Ln, bias=1e-5, scale=1.0), waits=[("sVE", cV3)])
        cE3 = op("scalar", lambda s: s.activation(
            out=stats[:, S_TMP + 12:S_TMP + 16],
            in_=stats[:, S_TMP + 8:S_TMP + 12],
            func=AF.Exp, bias=0.0, scale=-0.5))
        op("vector", lambda v: v.tensor_tensor(
            out=stats[:, S_TMP + 16:S_TMP + 20],
            in0=stats[:, S_TMP + 12:S_TMP + 16],
            in1=stats[:, S_TMP + 12:S_TMP + 16], op=ALU.mult),
           waits=[("sSE", cE3)])
        cV4 = op("vector", lambda v: v.tensor_tensor(
            out=stats[:, S_TMP + 16:S_TMP + 20],
            in0=stats[:, S_TMP + 4:S_TMP + 8],
            in1=stats[:, S_TMP + 16:S_TMP + 20], op=ALU.mult))
        cL4 = op("scalar", lambda s: s.activation(
            out=stats[:, S_TMP + 20:S_TMP + 24],
            in_=stats[:, S_TMP + 16:S_TMP + 20],
            func=AF.Ln, bias=1e-6, scale=1.0), waits=[("sVE", cV4)])
        cE4 = op("scalar", lambda s: s.activation(
            out=stats[:, S_TMP + 20:S_TMP + 24],
            in_=stats[:, S_TMP + 20:S_TMP + 24],
            func=AF.Exp, bias=0.0, scale=-0.5))
        op("vector", lambda v: v.tensor_tensor(
            out=stats[:, S_RHO:S_RHO + 4], in0=stats[:, S_TMP + 12:S_TMP + 16],
            in1=stats[:, S_TMP + 20:S_TMP + 24], op=ALU.mult),
           waits=[("sSE", cE4)])
        cRho = op("vector", lambda v: v.tensor_scalar(
            out=stats[:, S_RHO:S_RHO + 4], in0=stats[:, S_RHO:S_RHO + 4],
            scalar1=smalls[:, M_GFC:M_GFC + 1], scalar2=None, op0=ALU.mult),
            waits=[("dGAM", gam_done["fc"])])
        cNm3 = op("vector", lambda v: v.tensor_scalar(
            out=stats[:, S_NM3:S_NM3 + 4], in0=stats[:, S_TMP:S_TMP + 4],
            scalar1=-1.0, scalar2=None, op0=ALU.mult))
        xh_done = sch.now("sPE")
        xhat_buf = xhat_t if j == 0 else xh_ring
        xh_slot = {0: ("sVE", 0), 1: ("sSE", 0)}
        for ct in range(4):
            cXh = op("vector", (lambda v, c_=ct, b_=xhat_buf: v.tensor_scalar(
                out=b_, in0=GTx1[:, c_ * 2048:(c_ + 1) * 2048],
                scalar1=stats[:, S_NM3 + c_:S_NM3 + c_ + 1],
                scalar2=stats[:, S_RHO + c_:S_RHO + c_ + 1],
                op0=ALU.add, op1=ALU.mult)),
                waits=[("sVE", cRho), ("sVE", cNm3), ("sPE", xh_done)])
            for k in range(16):
                s2 = k % 2
                cTp = op("tensor", (lambda t, k_=k, s_=s2, b_=xhat_buf:
                                    t.transpose(
                    PBX[s_][:, 0:128],
                    b_[:, k_ * 128:(k_ + 1) * 128], identh[:])),
                    waits=[("sVE", cXh), xh_slot[s2]])
                if k % 2 == 0:
                    c = op("vector", (lambda v, k_=k, c_=ct, s_=s2:
                                      v.tensor_copy(
                        XhT[:, k_ * 512 + c_ * 128:k_ * 512 + c_ * 128 + 128],
                        PBX[s_][:, 0:128])), waits=[("sPE", cTp)])
                    xh_slot[s2] = ("sVE", c)
                else:
                    c = op("scalar", (lambda s, k_=k, c_=ct, s_=s2: s.copy(
                        XhT[:, k_ * 512 + c_ * 128:k_ * 512 + c_ * 128 + 128],
                        PBX[s_][:, 0:128])), waits=[("sPE", cTp)])
                    xh_slot[s2] = ("sSE", c)
            xh_done = sch.now("sPE")
        state["xh_done_ve"] = sch.now("sVE")
        state["xh_done_se"] = sch.now("sSE")
        state["gn_apply_done"] = gn_apply_done

    def emit_ffn_fc(j):
        gelu_done = {}
        sq_stt = {}
        sq_mm = {}
        fcw_free = {0: [("dOAC", state["dOAC"]), ("sVE", state["outb_free_ve"]),
                        ("sSE", state["outb_free_se"])],
                    1: [("sVE", state["xh_done_ve"]),
                        ("sSE", state["xh_done_se"]),
                        ("dOX1", sch.now("dOX1"))]}
        mm = 0
        mm_chb = {}

        def emit_stats(chb):
            sq_mm[chb] = op("tensor", (lambda t, c_=chb: t.matmul(
                PP[6][0:1, :], onesh[:], garena[:, c_ * 512:(c_ + 1) * 512],
                start=(c_ == 0), stop=(c_ == 63))),
                waits=[("sSE", gelu_done[chb])] +
                      ([("sVE", state["xtev"][0])] if chb == 0 else []))
            sq_mm[chb] = op("tensor", (lambda t, c_=chb: t.matmul(
                PP[6][32:33, :], onesh[:], sqring[c_ % 8][:],
                start=(c_ == 0), stop=(c_ == 63))),
                waits=[("sVE", sq_stt[chb])])

        for ob in range(16):
            if ob >= 2:
                for cq in range(4):
                    emit_stats(4 * (ob - 2) + cq)
            cWl = 0
            for kq in range(4):
                cWl = dma("sync", "dWF%d" % (ob % 2),
                          fcw8[ob % 2][:, kq * 2048:(kq + 1) * 2048].rearrange(
                              "p (k c) -> p k c", c=512),
                          scr_fc[ob, 4 * kq:4 * kq + 4].rearrange(
                              "k p c -> p k c"),
                          waits=(fcw_free[ob % 2]
                                 + [("dSCR0", sch.now("dSCR0")),
                                    ("dSCR1", sch.now("dSCR1"))]
                                 if kq == 0 else []))
            for k in range(16):
                for cq in range(4):
                    chb = 4 * ob + cq
                    bank = chb % 6
                    wv = [("dWF%d" % (ob % 2), cWl),
                          ("sVE", state["xh_done_ve"]),
                          ("sSE", state["xh_done_se"])]
                    if chb >= 6 and k == 0:
                        wv.append(("sSE", gelu_done[chb - 6]))
                    mm = op("tensor", (lambda t, k_=k, q_=cq, o_=ob, b_=bank:
                                       t.matmul(
                        PP[b_][:, :],
                        fcw8[o_ % 2][:, k_ * 512 + q_ * 128:
                                     k_ * 512 + q_ * 128 + 128],
                        XhT[:, k_ * 512:(k_ + 1) * 512],
                        start=(k_ == 0), stop=(k_ == 15))), waits=wv)
                    if k == 15:
                        mm_chb[chb] = mm
            fcw_free[ob % 2] = [("sPE", mm)]
            for cq in range(4):
                chb = 4 * ob + cq
                bank = chb % 6
                gelu_done[chb] = op("scalar", (lambda s, c_=chb, b_=bank:
                                              s.activation(
                    out=garena[:, c_ * 512:(c_ + 1) * 512],
                    in_=PP[b_][:, :], func=AF.Gelu)),
                    waits=[("sPE", mm_chb[chb])] +
                          (xbws if chb == 0 and j == 0 else []))
                wsq = [("sSE", gelu_done[chb])]
                if chb < 8:
                    wsq += state["dF_last"]
                    wsq += [("dOX1", sch.now("dOX1")),
                            ("sPE", state["fc_pe_last"])]
                else:
                    wsq += [("sPE", sq_mm[chb - 8])]
                sq_stt[chb] = op("vector", (lambda v, c_=chb:
                                            v.scalar_tensor_tensor(
                    out=sqring[c_ % 8][:],
                    in0=garena[:, c_ * 512:(c_ + 1) * 512], scalar=1.0,
                    in1=garena[:, c_ * 512:(c_ + 1) * 512],
                    op0=ALU.mult, op1=ALU.mult)), waits=wsq)
        for cq in range(4):
            emit_stats(56 + cq)
        for cq in range(4):
            emit_stats(60 + cq)
        state["gelu_last"] = gelu_done[63]
        state["fc_pe_last"] = sch.now("sPE")
        return gelu_done

    def emit_wcol():
        wfree = {0: [("sVE", state["xres_free"])],
                 1: [("sVE", state["xres_free"])]}
        prevW = 0
        for cb in range(4):
            mm = 0
            for og in range(8):
                cPl = dma("sync", "dWP%d" % (og % 2),
                          pjw8[og % 2][:].rearrange("p (o c) -> p o c", c=512),
                          scr_pj[cb, 8 * og:8 * og + 8].rearrange(
                              "o p c -> p o c"),
                          waits=(wfree[og % 2]
                                 + [("dSCR2", sch.now("dSCR2")),
                                    ("dSCR3", sch.now("dSCR3"))]))
                for o in range(8):
                    wrow = [PP[7][0:1, :], PP[7][32:33, :], PP[7][64:65, :],
                            PP[6][64:65, :]][cb]
                    mm = op("tensor", (lambda t, w_=wrow, o_=o, s_=og % 2,
                                       st_=(og == 0 and o == 0),
                                       sp_=(og == 7 and o == 7): t.matmul(
                        w_, onesh[:],
                        pjw8[s_][:, o_ * 512:(o_ + 1) * 512],
                        start=st_, stop=sp_)),
                        waits=[("dWP%d" % (og % 2), cPl)] +
                              ([("sVE", state["xtev"][0])]
                               if cb == 0 and og == 0 else []))
                wfree[og % 2] = [("sPE", mm)]
            wrow = [PP[7][0:1, :], PP[7][32:33, :], PP[7][64:65, :],
                    PP[6][64:65, :]][cb]
            cCp = op("vector", (lambda v, w_=wrow: v.tensor_copy(
                ROWR[6][:], w_)),
                waits=[("sPE", mm), ("dSTAT", prevW)])
            prevW = dma("sync", "dSTAT", wcol_dram[cb:cb + 1, :],
                        ROWR[6][:], waits=[("sVE", cCp)])
        state["wcol_done"] = prevW
        state["wcol_pe"] = sch.now("sPE")

    def emit_ffn_mid(j):
        w0 = [("sPE", state["fc_pe_last"])]
        op("vector", lambda v: v.tensor_scalar(
            out=ROWR[2][:], in0=PP[6][0:1, :], scalar1=1.0 / 8192,
            scalar2=None, op0=ALU.mult), waits=w0)
        cEq = op("vector", lambda v: v.tensor_scalar(
            out=ROWR[3][:], in0=PP[6][32:33, :], scalar1=1.0 / 8192,
            scalar2=None, op0=ALU.mult))
        op("vector", lambda v: v.tensor_tensor(
            out=ROWR[5][:], in0=ROWR[2][:], in1=ROWR[2][:],
            op=ALU.mult))
        cVr = op("vector", lambda v: v.tensor_tensor(
            out=ROWR[3][:], in0=ROWR[3][:], in1=ROWR[5][:],
            op=ALU.subtract))
        cLn = op("scalar", lambda s: s.activation(
            out=ROWR[5][:], in_=ROWR[3][:], func=AF.Ln, bias=1e-6,
            scale=1.0), waits=[("sVE", cVr)])
        cEx = op("scalar", lambda s: s.activation(
            out=ROWR[3][:], in_=ROWR[5][:], func=AF.Exp, bias=0.0,
            scale=-0.5))
        cRh = op("vector", lambda v: v.tensor_scalar(
            out=ROWR[3][:], in0=ROWR[3][:],
            scalar1=smalls[0:1, M_GPJ:M_GPJ + 1], scalar2=None, op0=ALU.mult),
            waits=[("sSE", cEx), ("dGAM", gam_done["pj"])])
        cMuro = op("vector", lambda v: v.tensor_scalar(
            out=ROWR[4][:], in0=ROWR[2][:], scalar1=-1.0, scalar2=None,
            op0=ALU.mult))
        c1 = dma("sync", "dSTAT", stat_scr[0:1, :], ROWR[3][:],
                 waits=[("sVE", cRh)])
        c2 = dma("sync", "dSTAT", stats[:, S_RH2:S_RH2 + 4],
                 stat_scr[0].rearrange("(s p) -> p s", p=128),
                 waits=[("dSTAT", c1)], slow=True)
        state["rho_col"] = c2
        state["muro_ve"] = cMuro
        state["fcstat_evac"] = cMuro

    def emit_ffn_proj(j, gelu_done):
        pj_evac = {}
        pjw_free = {0: [("sVE", state["xres_free"]),
                        ("sPE", state["wcol_pe"])],
                    1: [("sVE", state["xres_free"]),
                        ("sPE", state["wcol_pe"])]}
        fst_free = {0: ("dOX1", sch.now("dOX1")),
                    1: ("dOX1", sch.now("dOX1"))}
        wsl = {}
        rank1 = {}
        for cb in range(4):
            wsl[cb] = dma("scalar", "dWCL" if cb % 2 == 0 else "dWCL1",
                          ROWR[6 + cb % 2][:],
                          wcol_dram[cb:cb + 1, :],
                          waits=([("sPE", rank1.get(cb - 2, 0))]
                                 + ([("dSTAT", sch.now("dSTAT"))]
                                    if cb < 2 else [])))
            mm = 0
            for og in range(8):
                cPl = dma("sync", "dWP%d" % (og % 2),
                          pjw8[og % 2][:].rearrange("p (o c) -> p o c", c=512),
                          scr_pj[cb, 8 * og:8 * og + 8].rearrange(
                              "o p c -> p o c"),
                          waits=(pjw_free[og % 2]
                                 + [("dSCR2", sch.now("dSCR2")),
                                    ("dSCR3", sch.now("dSCR3"))]))
                for o in range(8 * og, 8 * og + 8):
                    for st in range(4):
                        bank = (4 * cb + st) % 6
                        wv = [("dWP%d" % (og % 2), cPl),
                              ("sSE", gelu_done[o]),
                              ("sVE", state["xtev"][0]),
                              ("sSE", state["xtev"][1])]
                        if 4 * cb + st >= 6 and o == 0:
                            cb2, st2 = divmod(4 * cb + st - 6, 4)
                            wv.append(("sVE", pj_evac[(cb2, st2)]))
                        mm = op("tensor", (lambda t, o_=o, s_=st, b_=bank:
                                           t.matmul(
                            PP[b_][:, :],
                            garena[:, o_ * 512 + s_ * 128:
                                   o_ * 512 + s_ * 128 + 128],
                            pjw8[(o_ // 8) % 2][:, (o_ % 8) * 512:
                                                (o_ % 8) * 512 + 512],
                            start=(o_ == 0), stop=False)), waits=wv)
                pjw_free[og % 2] = [("sPE", mm)]
            for st in range(4):
                bank = (4 * cb + st) % 6
                mm = op("tensor", (lambda t, s_=st, b_=bank, c_=cb: t.matmul(
                    PP[b_][:, :], ROWR[4][:, s_ * 128:(s_ + 1) * 128],
                    ROWR[6 + c_ % 2][:],
                    start=False, stop=True)),
                    waits=[("sVE", state["muro_ve"]),
                           ("dWCL" if cb % 2 == 0 else "dWCL1", wsl[cb])])
                rank1[cb] = mm
            for st in range(4):
                bank = (4 * cb + st) % 6
                slot = (cb * 4 + st) % 2
                pj_evac[(cb, st)] = op("vector", (lambda v, s_=st, b_=bank,
                                                  sl_=slot, c_=cb:
                                                  v.tensor_scalar(
                    out=fstg[sl_], in0=PP[b_][:, :],
                    scalar1=stats[:, S_RH2 + s_:S_RH2 + s_ + 1], scalar2=None,
                    op0=ALU.mult, op1=ALU.add,
                    accum_out=stats[:, S_FS + s_ * 4 + c_:
                                    S_FS + s_ * 4 + c_ + 1])),
                    waits=[("sPE", rank1[cb]), fst_free[slot],
                           ("dSTAT", state["rho_col"])])
                cw = dma("gpsimd", "dF%d" % slot,
                         f_dram[st, :, cb * 512:(cb + 1) * 512],
                         fstg[slot], waits=[("sVE", pj_evac[(cb, st)])])
                fst_free[slot] = ("dF%d" % slot, cw)
        state["proj_mm_last"] = sch.now("sPE")
        state["pjevac_last"] = sch.now("sVE")
        state["dF_last"] = [("dF0", sch.now("dF0")), ("dF1", sch.now("dF1"))]
        return state["dF_last"]

    def emit_ffn_out(j, f_written):
        op("vector", lambda v: v.tensor_reduce(
            out=stats[:, S_F4:S_F4 + 4],
            in_=stats[:, S_FS:S_FS + 16].rearrange("p (a b) -> p a b", b=4),
            axis=AX, op=ALU.add), waits=[("sVE", state["pjevac_last"])])
        ac = 0
        prev_consumed = 0
        for st in range(4):
            w = list(f_written)
            if st == 0:
                w += [("dOAC", state["dOAC"]), ("sVE", state["outb_free_ve"]),
                      ("sSE", state["outb_free_se"])]
            else:
                w += [("sVE", prev_consumed), ("dOAC", ac)]
            cRd = dma("sync", "dFR", outb[:], f_dram[st], waits=w)
            fq = 0
            for qq in range(4):
                wq_ = [("dFR", cRd)]
                if st == 0 and qq == 0:
                    wq_ += [("sVE", state["dump_last"]),
                            ("sSE", state["dump_se"])]
                fq = op("vector", (lambda v, q_=qq, s_=st:
                                   v.scalar_tensor_tensor(
                    out=gdump[:], in0=outb[:, q_ * 512:(q_ + 1) * 512],
                    scalar=1.0, in1=outb[:, q_ * 512:(q_ + 1) * 512],
                    op0=ALU.mult, op1=ALU.mult,
                    accum_out=stats[:, 384 + s_ * 4 + q_:
                                    384 + s_ * 4 + q_ + 1])),
                    waits=wq_)
            state["dump_last"] = fq
            op("vector", (lambda v, s_=st: v.tensor_reduce(
                out=stats[:, S_TM2 + 44 + s_:S_TM2 + 44 + s_ + 1],
                in_=stats[:, 384 + s_ * 4:384 + s_ * 4 + 4].rearrange(
                    "p (a b) -> p a b", b=4),
                axis=AX, op=ALU.add)), waits=[("sVE", fq)])
            op("vector", (lambda v, s_=st: v.tensor_scalar(
                out=stats[:, S_TM2 + 48 + s_:S_TM2 + 48 + s_ + 1],
                in0=stats[:, S_TM2 + 44 + s_:S_TM2 + 44 + s_ + 1],
                scalar1=1.0 / 2048, scalar2=None, op0=ALU.mult)))
            op("vector", (lambda v, s_=st: v.tensor_scalar(
                out=stats[:, S_TM2 + 52 + s_:S_TM2 + 52 + s_ + 1],
                in0=stats[:, S_F4 + s_:S_F4 + s_ + 1],
                scalar1=1.0 / 2048, scalar2=None, op0=ALU.mult)))
            op("vector", (lambda v, s_=st: v.tensor_tensor(
                out=stats[:, S_TM2 + 56 + s_:S_TM2 + 56 + s_ + 1],
                in0=stats[:, S_TM2 + 52 + s_:S_TM2 + 52 + s_ + 1],
                in1=stats[:, S_TM2 + 52 + s_:S_TM2 + 52 + s_ + 1],
                op=ALU.mult)))
            cVf = op("vector", (lambda v, s_=st: v.tensor_tensor(
                out=stats[:, S_TM2 + 48 + s_:S_TM2 + 48 + s_ + 1],
                in0=stats[:, S_TM2 + 48 + s_:S_TM2 + 48 + s_ + 1],
                in1=stats[:, S_TM2 + 56 + s_:S_TM2 + 56 + s_ + 1],
                op=ALU.subtract)))
            cLf = op("scalar", (lambda s, s_=st: s.activation(
                out=stats[:, S_TM2 + 56 + s_:S_TM2 + 56 + s_ + 1],
                in_=stats[:, S_TM2 + 48 + s_:S_TM2 + 48 + s_ + 1],
                func=AF.Ln, bias=1e-5, scale=1.0)), waits=[("sVE", cVf)])
            cEf = op("scalar", (lambda s, s_=st: s.activation(
                out=stats[:, S_TM2 + 60 + s_:S_TM2 + 60 + s_ + 1],
                in_=stats[:, S_TM2 + 56 + s_:S_TM2 + 56 + s_ + 1],
                func=AF.Exp, bias=0.0, scale=-0.5)))
            cNm4 = op("vector", (lambda v, s_=st: v.tensor_scalar(
                out=stats[:, S_TM2 + 64 + s_:S_TM2 + 64 + s_ + 1],
                in0=stats[:, S_TM2 + 52 + s_:S_TM2 + 52 + s_ + 1],
                scalar1=-1.0, scalar2=None, op0=ALU.mult)),
                waits=[("sSE", cEf)])
            cO1 = op("vector", (lambda v, s_=st: v.tensor_scalar(
                out=outb[:], in0=outb[:],
                scalar1=stats[:, S_TM2 + 64 + s_:S_TM2 + 64 + s_ + 1],
                scalar2=stats[:, S_TM2 + 60 + s_:S_TM2 + 60 + s_ + 1],
                op0=ALU.add, op1=ALU.mult)))
            prev_consumed = cO1
            r0 = j + 1024 * st
            ac = dma("gpsimd", "dOAC", outd[r0:r0 + 1017:8, :], outb[:],
                     waits=[("sVE", cO1), ("dOX1", sch.now("dOX1"))],
                     accum=True)
            state["outb_free_ve"] = cO1
        state["dOAC"] = ac

    emit_gfin("fc")
    emit_tern("fc")
    emit_gamma_pj()
    emit_gfin("pj")
    cd0 = emit_attn_mm(0)
    emit_attn_tail(0, cd0)
    emit_tern("pj")
    state["stg_handoff"] = [("sSE", sch.now("sSE")), ("sVE", sch.now("sVE")),
                            ("dSCR0", sch.now("dSCR0")),
                            ("dSCR1", sch.now("dSCR1")),
                            ("dSCR2", sch.now("dSCR2")),
                            ("dSCR3", sch.now("dSCR3"))]
    for j in range(NCH):
        g = emit_ffn_fc(j)
        if j == 0:
            emit_wcol()
        emit_ffn_mid(j)
        if j + 1 < NCH:
            cd = emit_attn_mm(j + 1)
        fw = emit_ffn_proj(j, g)
        emit_ffn_out(j, fw)
        if j + 1 < NCH:
            emit_attn_tail(j + 1, cd)

    sch.emit("sync", None, waits=[(s, sch.now(s)) for s in DSEMS
                                  if sch.now(s) > 0])

    with ctx:
        with nc.Block() as block:
            def runner(ename):
                def _(eng):
                    for (waits, fn, inc) in sch.ev[ename]:
                        for (s, v) in waits:
                            eng.wait_ge(sem[s], v)
                        if fn is None:
                            continue
                        ins = fn(eng)
                        if inc is not None:
                            ins.then_inc(sem[inc],
                                         16 if inc.startswith("d") else 1)
                return _
            block.sync(runner("sync"))
            block.scalar(runner("scalar"))
            block.gpsimd(runner("gpsimd"))
            block.tensor(runner("tensor"))
            block.vector(runner("vector"))
    return nc


_NC_CACHE = None


def kernel(L0_attn_X, L0_attn_S_n, L0_attn_W_Q, L0_attn_hh_b, L0_attn_gn_weight,
           L0_attn_gn_bias, L0_ffn_w_fc, L0_ffn_w_proj):
    global _NC_CACHE
    assert np.asarray(L0_attn_S_n).shape == (2, 2048)
    X = np.ascontiguousarray(np.asarray(L0_attn_X, dtype=np.float32))
    WQ = np.ascontiguousarray(np.asarray(L0_attn_W_Q, dtype=np.float32))
    hh = np.asarray(L0_attn_hh_b, dtype=np.float32)
    gnw = np.asarray(L0_attn_gn_weight, dtype=np.float32)
    wfc = np.asarray(L0_ffn_w_fc, dtype=np.float32)
    wpj = np.asarray(L0_ffn_w_proj, dtype=np.float32)

    wfcT = np.ascontiguousarray(wfc.T)
    wpjT = np.ascontiguousarray(wpj.T)

    t64 = np.arange(T, dtype=np.float64)
    inv = 10000.0 ** (-np.arange(0, 64, 2, dtype=np.float64) / 64.0)
    fr = np.repeat(t64[:, None] * inv[None, :], 2, axis=1)
    cosT = np.ascontiguousarray(np.cos(fr).astype(np.float32))
    sinT = np.ascontiguousarray(np.sin(fr).astype(np.float32))
    cosI = np.ones_like(cosT)
    sinI = np.zeros_like(sinT)
    identf = np.eye(128, dtype=np.float32)
    identh = np.eye(128, dtype=np.float32).astype(ml_dtypes.bfloat16)
    onesh = np.ones((128, 1), dtype=np.float32).astype(ml_dtypes.bfloat16)
    onesrow = np.ones((1, 128), dtype=np.float32)
    constv_np = np.tile(np.array(CONSTVALS, dtype=np.float32), (128, 1))

    if _NC_CACHE is None:
        _NC_CACHE = build_nc()
    nc = _NC_CACHE

    in_maps = []
    for core in range(8):
        b, m = core // 4, core % 4
        cols = slice(512 * m, 512 * (m + 1))
        in_maps.append({
            "xfull": X[b],
            "xres": np.ascontiguousarray(X[b, 4096 * m:4096 * (m + 1), :]),
            "wq": np.ascontiguousarray(WQ[:, cols]),
            "hhin": np.ascontiguousarray(hh[cols]).reshape(1, 512),
            "gnwin": np.ascontiguousarray(gnw[cols]).reshape(1, 512),
            "ropec": cosT if m == 0 else cosI,
            "ropes": sinT if m == 0 else sinI,
            "wfcT": wfcT,
            "wpjT": wpjT,
            "identf": identf,
            "identh": identh,
            "onesh": onesh,
            "onesrow": onesrow,
            "constv": constv_np,
        })
    res = run_bass_kernel_spmd(nc, in_maps, core_ids=list(range(8)))
    out = np.empty((2, T, C), dtype=np.float32)
    for core in range(8):
        b, m = core // 4, core % 4
        out[b, 4096 * m:4096 * (m + 1), :] = res.results[core]["out"]
    return out
